# version 01
# baseline (speedup 1.0000x reference)
import contextlib
import numpy as np
import ml_dtypes
import concourse.bass as bass
import concourse.mybir as mybir
from concourse.bass_utils import run_bass_kernel_spmd

F32 = mybir.dt.float32
BF16 = mybir.dt.bfloat16
AF = mybir.ActivationFunctionType
ALU = mybir.AluOpType
AX = mybir.AxisListType

B, S, HID = 4, 4096, 4096
H, G, D = 32, 8, 128
NST = 32
MYST = 16
ST = 128
NKT = 32
THETA = 10000.0
EPS = 1e-5

SET_A = sorted([4 * k for k in range(8)] + [4 * k + 3 for k in range(8)])
SET_B = sorted([4 * k + 1 for k in range(8)] + [4 * k + 2 for k in range(8)])
for _j, _p in enumerate(SET_A):
    assert _p in (2 * _j, 2 * _j + 1)
for _j, _p in enumerate(SET_B):
    assert _p in (2 * _j, 2 * _j + 1)

ENGS = "TVSQG"


class Sem:
    __slots__ = ("h", "n")

    def __init__(self, h):
        self.h = h
        self.n = 0


class Prog:

    def __init__(self, nc, es):
        self.nc = nc
        self.es = es
        self.st = {k: [] for k in ENGS}
        self.sems = []

    def sem(self, name):
        s = Sem(self.es.enter_context(self.nc.semaphore(name)))
        self.sems.append(s)
        return s

    def barrier(self):
        for eng in ENGS:
            for s in self.sems:
                if s.n > 0:
                    self.w(eng, s, s.n)

    def w(self, eng, sem, val):
        if val > 0:
            self.st[eng].append(lambda e, s=sem, v=val: e.wait_ge(s.h, v))

    def do(self, eng, fn, inc=None, amt=1):
        if inc is not None:
            inc.n += amt
        self.st[eng].append(
            lambda e, f=fn, i=inc, a=amt:
            (f(e).then_inc(i.h, a) if i is not None else f(e)))
        return inc.n if inc is not None else None


SB0 = 16384


def _sb_at(nc, name, shape, dtype, off_bytes):
    return nc.alloc_sbuf_tensor_at(name, shape, dtype, offset=SB0 + off_bytes)


def build_program(dbg=False, upto=99):
    nc = bass.Bass("TRN2", target_bir_lowering=False, debug=False)
    es = contextlib.ExitStack()
    p = Prog(nc, es)
    din = lambda n, sh, dt: nc.dram_tensor(n, sh, dt, kind="ExternalInput").ap()
    dout = lambda n, sh, dt: nc.dram_tensor(n, sh, dt, kind="ExternalOutput").ap()
    dscr = lambda n, sh, dt: nc.dram_tensor(
        n, sh, dt, kind="ExternalOutput" if dbg else "Internal").ap()

    x_full = din("x_full", [S, HID], F32)
    x_mine = din("x_mine", [MYST * ST, HID], F32)
    Wq = din("Wq", [HID, H * D], F32)
    Wk = din("Wk", [HID, G * D], F32)
    Wv = din("Wv", [HID, G * D], F32)
    Wo = din("Wo", [H * D, HID], F32)
    Wf = din("Wf", [HID, HID], F32)
    bqA_r = din("bqA_r", [16, 128], F32)
    bqB_r = din("bqB_r", [16, 128], F32)
    bkA_r = din("bkA_r", [4, 128], F32)
    bkB_r = din("bkB_r", [4, 128], F32)
    bv_i = din("bv_i", [1, G * D], F32)
    bo_i = din("bo_i", [1, HID], F32)
    fb_i = din("fb_i", [1, HID], F32)
    tabKA = din("tabKA", [128, S], F32)
    tabKB = din("tabKB", [128, S], F32)
    tabQA = din("tabQA", [128, MYST * ST], F32)
    tabQB = din("tabQB", [128, MYST * ST], F32)
    dmask_i = din("dmask_i", [128, MYST * 2 * 128], BF16)
    identf = din("identf", [128, 128], F32)
    onescol_i = din("onescol_i", [128, 1], BF16)
    onesrow_bf_i = din("onesrow_bf_i", [1, 128], BF16)
    onesrow_f_i = din("onesrow_f_i", [1, 128], F32)
    out_c = dout("out_c", [MYST * ST, HID], F32)

    xbf = dscr("xbf", [S, HID], BF16)
    xbfm = dscr("xbfm", [MYST * ST, HID], BF16)
    wq_bf = dscr("wq_bf", [H * D, HID], BF16)
    wo_bf = dscr("wo_bf", [8 * 128, NKT * 512], BF16)
    kT = dscr("kT", [G * 128, S], BF16)
    v2 = dscr("v2", [S, G * D], BF16)
    qTs = dscr("qTs", [H * 128, MYST * ST], BF16)
    ctxT = dscr("ctxT", [H * 128, MYST * ST], BF16)
    x1pre = dscr("x1pre", [MYST * ST, HID], F32)
    x1row = dscr("x1row", [MYST * ST, HID], BF16)
    qrow = dscr("qrow", [HID, HID], BF16)
    qWT = dscr("qWT", [NKT * 128, HID], BF16)
    x2pre = dscr("x2pre", [MYST * ST, HID], F32)

    KB = 1024
    cid = _sb_at(nc, "cid", [128, 128], F32, 0)
    onescol = _sb_at(nc, "onescol", [128, 1], BF16, 512)
    onesrow_bf = _sb_at(nc, "onesrow_bf", [1, 128], BF16, 544)
    onesrow_f = _sb_at(nc, "onesrow_f", [1, 128], F32, 800)
    rows = _sb_at(nc, "rows", [1, 2048], F32, 1312)
    bqtA = _sb_at(nc, "bqtA", [128, 16], F32, 9504)
    bqtB = _sb_at(nc, "bqtB", [128, 16], F32, 9568)
    bktA = _sb_at(nc, "bktA", [128, 4], F32, 9632)
    bktB = _sb_at(nc, "bktB", [128, 4], F32, 9664)
    negUb = _sb_at(nc, "negUb", [128, 64], F32, 9696)
    dmask = _sb_at(nc, "dmask", [128, MYST * 2 * 128], BF16, 10176)
    bvb = _sb_at(nc, "bvb", [128, G * D], F32, 18368)
    sums_all = _sb_at(nc, "sums_all", [128, 64], F32, 22464)
    sq_all = _sb_at(nc, "sq_all", [128, 64], F32, 22720)
    stt1 = _sb_at(nc, "stt1", [128, 8], F32, 22976)
    stt2 = _sb_at(nc, "stt2", [128, 8], F32, 23008)
    stt3 = _sb_at(nc, "stt3", [128, 8], F32, 23040)
    stt4 = _sb_at(nc, "stt4", [128, 8], F32, 23072)
    BASE = 24 * KB

    qmaxrow = rows[0:1, 0:64]
    km2 = rows[0:1, 128:129]
    rtmp1 = rows[0:1, 256:320]
    rtmp2 = rows[0:1, 384:448]
    negu_row = rows[0:1, 512:576]
    rec_row = rows[0:1, 1024:1536]
    negu_bf = _sb_at(nc, "negu_bf", [1, 64], BF16, 23104)
    kmx_t = rows[0:1, 129:130]

    pA = es.enter_context(nc.psum_tensor("pA", [128, 1024], F32))
    pB = es.enter_context(nc.psum_tensor("pB", [128, 1024], F32))
    pC = es.enter_context(nc.psum_tensor("pC", [128, 512], F32))
    pD = es.enter_context(nc.psum_tensor("pD", [128, 512], F32))
    pE = es.enter_context(nc.psum_tensor("pE", [128, 512], F32))
    pF = es.enter_context(nc.psum_tensor("pF", [128, 512], F32))
    pAh = [pA[:, 0:512], pA[:, 512:1024]]
    pBh = [pB[:, 0:512], pB[:, 512:1024]]
    p4rot = [pA[:, 0:512], pA[:, 512:1024], pB[:, 0:512], pB[:, 512:1024]]

    gI = p.sem("gI")
    tI = p.sem("tI")
    vI = p.sem("vI")

    def dma(eng, dst, src, sem, **kw):
        return p.do(eng, lambda e, d=dst, s=src: e.dma_start(d, s, **kw),
                    sem, 16)

    dma("G", cid[:], identf, gI)
    dma("G", onescol[:], onescol_i, gI)
    dma("G", onesrow_bf[:], onesrow_bf_i, gI)
    dma("G", onesrow_f[:], onesrow_f_i, gI)
    dma("G", dmask[:], dmask_i, gI)
    btmp = _sb_at(nc, "btmp", [128, 128], F32, BASE)
    dma("G", btmp[0:16, :], bqA_r, gI)
    dma("G", btmp[16:32, :], bqB_r, gI)
    dma("G", btmp[32:36, :], bkA_r, gI)
    dma("G", btmp[36:40, :], bkB_r, gI)
    dma("G", bvb[:], bass.AP(bv_i.tensor, 0, [[0, 128], [1, G * D]]), gI)
    g_consts = gI.n
    p.w("V", gI, g_consts)
    p.do("V", lambda e: e.memset(rows[:], 0.0), vI)
    p.do("V", lambda e: e.memset(negUb[:], 0.0), vI)
    p.w("T", gI, g_consts)
    p.w("T", vI, vI.n)
    p.do("T", lambda e: e.transpose(pF[:, 0:128], btmp[:], cid[:]), tI)
    p.w("V", tI, tI.n)
    p.do("V", lambda e: e.tensor_copy(bqtA[:], pF[:, 0:16]), vI)
    p.do("V", lambda e: e.tensor_copy(bqtB[:], pF[:, 16:32]), vI)
    p.do("V", lambda e: e.tensor_copy(bktA[:], pF[:, 32:36]), vI)
    p.do("V", lambda e: e.tensor_copy(bktB[:], pF[:, 36:40]), vI)
    v_binit = vI.n

    xbf_done = []
    for tc in range(8):
        dma("G", xbf[tc * 512:(tc + 1) * 512, :], x_full[tc * 512:(tc + 1) * 512, :], gI)
        xbf_done.append(gI.n)
    xbfm_done = []
    for sc in range(4):
        dma("G", xbfm[sc * 512:(sc + 1) * 512, :], x_mine[sc * 512:(sc + 1) * 512, :], gI)
        xbfm_done.append(gI.n)
    wk_cast0 = gI.n

    if upto >= 1:
        t1 = p.sem("t1")
        v1 = p.sem("v1")
        s1 = p.sem("s1")
        q1 = p.sem("q1")
        g1 = p.sem("g1")
        wk_sb = _sb_at(nc, "wk_sb", [128, NKT * 1024], BF16, BASE)
        xt_sb = [_sb_at(nc, f"xt{i}", [128, NKT * 512], BF16, 88 * KB + i * 32 * KB)
                 for i in range(2)]
        tka_sb = [_sb_at(nc, f"tka{i}", [128, 512], F32, 152 * KB + i * 2 * KB) for i in range(2)]
        tkb_sb = [_sb_at(nc, f"tkb{i}", [128, 512], F32, 156 * KB + i * 2 * KB) for i in range(2)]
        klo_sb = [_sb_at(nc, f"klo{i}", [128, 512], BF16, 160 * KB + i * KB) for i in range(2)]
        khi_sb = [_sb_at(nc, f"khi{i}", [128, 512], BF16, 162 * KB + i * KB) for i in range(2)]
        ksqlo = [_sb_at(nc, f"ksqlo{i}", [128, 512], BF16, 164 * KB + i * KB) for i in range(2)]
        ksqhi = [_sb_at(nc, f"ksqhi{i}", [128, 512], BF16, 166 * KB + i * KB) for i in range(2)]
        ktmp1 = _sb_at(nc, "ktmp1", [128, 512], F32, 168 * KB)
        ktmp2 = _sb_at(nc, "ktmp2", [128, 512], F32, 170 * KB)

        p.w("G", tI, tI.n)
        for h2 in range(2):
            dma("G", wk_sb[:].rearrange("p (kt n) -> p kt n", n=1024)[:, h2 * 16:(h2 + 1) * 16, :],
                Wk.rearrange("(kt p) n -> p kt n", p=128)[:, h2 * 16:(h2 + 1) * 16, :], gI)
        wk_done = gI.n

        xt_cons = [0, 0]
        bank_ev = [0, 0]
        norm_ev = [0, 0]
        ko_st = [0, 0]
        tr_done = []
        for tc in range(8):
            b = tc % 2
            for kt in range(NKT):
                eng = "Q"
                p.w(eng, gI, xbf_done[tc])
                if tc >= 2:
                    p.w(eng, t1, xt_cons[b])
                p.do(eng, lambda e, kt=kt, tc=tc, b=b: e.dma_start_transpose(
                    xt_sb[b][:, kt * 512:(kt + 1) * 512],
                    xbf[tc * 512:(tc + 1) * 512, kt * 128:(kt + 1) * 128]), q1, 16)
            tr_done.append(q1.n)
            dma("Q", tka_sb[b][:], tabKA[:, tc * 512:(tc + 1) * 512], q1)
            dma("Q", tkb_sb[b][:], tabKB[:, tc * 512:(tc + 1) * 512], q1)
            tab_done = q1.n
            for gp in range(4):
                pp = (tc * 4 + gp) % 2
                ph = [pAh, pBh][pp]
                kb = (tc * 4 + gp) % 2
                p.w("T", q1, tr_done[-1])
                p.w("T", gI, wk_done)
                p.w("T", v1, bank_ev[pp])
                for half in range(2):
                    nt2 = gp * 2 + half
                    for kt in range(NKT):
                        n = p.do("T", lambda e, nt2=nt2, kt=kt, half=half, pp=pp, b=b:
                                 e.matmul(
                                     ph[half] if False else [pAh, pBh][pp][half],
                                     wk_sb[:, kt * 1024 + nt2 * 128:kt * 1024 + (nt2 + 1) * 128],
                                     xt_sb[b][:, kt * 512:(kt + 1) * 512],
                                     start=(kt == 0), stop=(kt == NKT - 1)),
                                 t1 if kt == NKT - 1 else None)
                        if kt == NKT - 1:
                            mm_done = n
                xt_cons[b] = mm_done
                p.w("V", t1, mm_done)
                p.w("V", q1, tab_done)
                p.w("V", g1, ko_st[kb])
                PA = [pAh, pBh][pp][0]
                PB = [pAh, pBh][pp][1]
                p.do("V", lambda e, gp=gp, PA=PA, b=b: e.scalar_tensor_tensor(
                    ktmp1[:], PA, bktA[:, gp:gp + 1], tka_sb[b][:], ALU.add, ALU.mult))
                p.do("V", lambda e, gp=gp, PB=PB, b=b: e.scalar_tensor_tensor(
                    ktmp2[:], PB, bktB[:, gp:gp + 1], tkb_sb[b][:], ALU.add, ALU.mult))
                p.do("V", lambda e, kb=kb: e.tensor_sub(klo_sb[kb][:], ktmp1[:], ktmp2[:]))
                p.do("V", lambda e, gp=gp, PA=PA, b=b: e.scalar_tensor_tensor(
                    ktmp1[:], PA, bktA[:, gp:gp + 1], tkb_sb[b][:], ALU.add, ALU.mult))
                p.do("V", lambda e, gp=gp, PB=PB, b=b: e.scalar_tensor_tensor(
                    ktmp2[:], PB, bktB[:, gp:gp + 1], tka_sb[b][:], ALU.add, ALU.mult))
                bank_ev[pp] = p.do("V", lambda e, kb=kb: e.tensor_add(
                    khi_sb[kb][:], ktmp1[:], ktmp2[:]), v1)
                ev_done = bank_ev[pp]
                p.w("S", v1, ev_done)
                p.do("S", lambda e, kb=kb: e.activation(ksqlo[kb][:], klo_sb[kb][:], AF.Square))
                sq_done = p.do("S", lambda e, kb=kb: e.activation(
                    ksqhi[kb][:], khi_sb[kb][:], AF.Square), s1)
                nb = [pD, pE][gp % 2]
                p.w("T", s1, sq_done)
                p.w("T", v1, norm_ev[gp % 2])
                p.do("T", lambda e, kb=kb, nb=nb: e.matmul(
                    nb[0:1, :], onescol[:], ksqlo[kb][:], start=True, stop=False,
                    skip_group_check=True))
                ones_done = p.do("T", lambda e, kb=kb, nb=nb: e.matmul(
                    nb[0:1, :], onescol[:], ksqhi[kb][:], start=False, stop=True,
                    skip_group_check=True), t1)
                p.w("V", t1, ones_done)
                p.w("V", vI, v_binit)
                rmx = p.do("V", lambda e, nb=nb: e.reduce_max(kmx_t, nb[0:1, :], axis=AX.X), v1)
                p.w("V", v1, rmx)
                norm_ev[gp % 2] = p.do("V", lambda e: e.tensor_max(km2, km2, kmx_t), v1)
                p.w("V", v1, norm_ev[gp % 2])
                p.w("G", v1, ev_done)
                for (buf, hoff) in ((klo_sb, 0), (khi_sb, 64)):
                    for two in range(2):
                        g = 2 * gp + two
                        ko_st[kb] = 16 + g1.n
                        dma("G", kT[g * 128 + hoff:g * 128 + hoff + 64,
                                   tc * 512:(tc + 1) * 512],
                            buf[kb][two * 64:(two + 1) * 64, :], g1)
        p1a_g_final = g1.n
        p1a_km_final = v1.n

    if upto >= 2:
        p.barrier()
        t2 = p.sem("t2")
        v2s = p.sem("v2s")
        q2 = p.sem("q2")
        g2 = p.sem("g2")
        wv_sb = _sb_at(nc, "wv_sb", [128, NKT * 1024], BF16, BASE)
        vbuf = [_sb_at(nc, f"vbuf{i}", [128, 512], BF16, 152 * KB + i * KB) for i in range(2)]
        for h2 in range(2):
            dma("G", wv_sb[:].rearrange("p (kt n) -> p kt n", n=1024)[:, h2 * 16:(h2 + 1) * 16, :],
                Wv.rearrange("(kt p) n -> p kt n", p=128)[:, h2 * 16:(h2 + 1) * 16, :], gI)
        wv_done = gI.n
        xt_cons = [0, 0]
        bank_ev = [0, 0]
        vb_st = [0, 0]
        for tc in range(8):
            b = tc % 2
            for kt in range(NKT):
                eng = "Q"
                p.w(eng, gI, xbf_done[tc])
                if upto >= 1 and tc < 2:
                    p.w(eng, t1, t1.n)
                if tc >= 2:
                    p.w(eng, t2, xt_cons[b])
                p.do(eng, lambda e, kt=kt, tc=tc, b=b: e.dma_start_transpose(
                    xt_sb[b][:, kt * 512:(kt + 1) * 512],
                    xbf[tc * 512:(tc + 1) * 512, kt * 128:(kt + 1) * 128]), q2, 16)
            trd = q2.n
            for ts in range(4):
                for nch in range(2):
                    i2 = (tc * 8 + ts * 2 + nch) % 2
                    p.w("T", q2, trd)
                    p.w("T", gI, wv_done)
                    p.w("T", v2s, bank_ev[i2])
                    for kt in range(NKT):
                        n = p.do("T", lambda e, ts=ts, nch=nch, kt=kt, i2=i2, b=b: e.matmul(
                            pAh[i2],
                            xt_sb[b][:, kt * 512 + ts * 128:kt * 512 + (ts + 1) * 128],
                            wv_sb[:, kt * 1024 + nch * 512:kt * 1024 + (nch + 1) * 512],
                            start=(kt == 0), stop=(kt == NKT - 1)),
                            t2 if kt == NKT - 1 else None)
                        if kt == NKT - 1:
                            mm_done = n
                    xt_cons[b] = mm_done
                    p.w("V", t2, mm_done)
                    p.w("V", g2, vb_st[i2])
                    bank_ev[i2] = p.do("V", lambda e, nch=nch, i2=i2: e.tensor_add(
                        vbuf[i2][:], pAh[i2], bvb[:, nch * 512:(nch + 1) * 512]), v2s)
                    p.w("G", v2s, bank_ev[i2])
                    vb_st[i2] = 16 + g2.n
                    dma("G", v2[tc * 512 + ts * 128:tc * 512 + (ts + 1) * 128,
                               nch * 512:(nch + 1) * 512], vbuf[i2][:], g2)
        p1b_g_final = g2.n

    if upto >= 3:
        p.barrier()
        t3 = p.sem("t3")
        v3 = p.sem("v3")
        s3 = p.sem("s3")
        q3 = p.sem("q3")
        g3 = p.sem("g3")
        xtm_sb = [_sb_at(nc, f"xtm{i}", [128, NKT * 512], BF16, BASE + i * 32 * KB)
                  for i in range(2)]
        wq_sb = [_sb_at(nc, f"wqsb{i}", [128, NKT * 128], BF16, 88 * KB + i * 8 * KB)
                 for i in range(4)]
        tqa_sb = [_sb_at(nc, f"tqa{i}", [128, 512], F32, 120 * KB + i * 2 * KB) for i in range(2)]
        tqb_sb = [_sb_at(nc, f"tqb{i}", [128, 512], F32, 124 * KB + i * 2 * KB) for i in range(2)]
        qtmp1 = _sb_at(nc, "qtmp1", [128, 512], F32, 128 * KB)
        qtmp2 = _sb_at(nc, "qtmp2", [128, 512], F32, 130 * KB)
        qlo_sb = [_sb_at(nc, f"qlo{i}", [128, 512], BF16, 132 * KB + i * KB) for i in range(2)]
        qhi_sb = [_sb_at(nc, f"qhi{i}", [128, 512], BF16, 134 * KB + i * KB) for i in range(2)]
        qsqlo = [_sb_at(nc, f"qsqlo{i}", [128, 512], BF16, 136 * KB + i * KB) for i in range(2)]
        qsqhi = [_sb_at(nc, f"qsqhi{i}", [128, 512], BF16, 138 * KB + i * KB) for i in range(2)]
        for nt in range(32):
            dma("G", wq_bf[nt * 128:(nt + 1) * 128, :].rearrange(
                    "p (kt n) -> p kt n", n=128),
                Wq.rearrange("(kt p) (nt n) -> nt p kt n", p=128, n=128)[nt], gI)
        wqc_done = gI.n
        xtm_cons = [0, 0]
        wq_cons = [0, 0, 0, 0]
        bank_ev = [0, 0]
        norm_ev = [0, 0]
        qo_st = [0, 0]
        for sc in range(4):
            b = sc % 2
            for kt in range(NKT):
                eng = "Q"
                p.w(eng, gI, xbfm_done[sc])
                if upto >= 2 and sc < 2:
                    p.w(eng, t2, t2.n)
                if sc >= 2:
                    p.w(eng, t3, xtm_cons[b])
                p.do(eng, lambda e, kt=kt, sc=sc, b=b: e.dma_start_transpose(
                    xtm_sb[b][:, kt * 512:(kt + 1) * 512],
                    xbfm[sc * 512:(sc + 1) * 512, kt * 128:(kt + 1) * 128]), q3, 16)
            trd = q3.n
            dma("Q", tqa_sb[b][:], tabQA[:, sc * 512:(sc + 1) * 512], q3)
            dma("Q", tqb_sb[b][:], tabQB[:, sc * 512:(sc + 1) * 512], q3)
            tabd = q3.n
            for hp in range(16):
                wbs = [(hp % 2) * 2, (hp % 2) * 2 + 1]
                for half in range(2):
                    nt2 = 2 * hp + half
                    p.w("Q", gI, wqc_done)
                    p.w("Q", t3, wq_cons[wbs[half]])
                    dma("Q", wq_sb[wbs[half]][:], wq_bf[nt2 * 128:(nt2 + 1) * 128, :], q3)
                wqld = q3.n
                pp = hp % 2
                p.w("T", q3, max(trd, wqld))
                p.w("T", v3, bank_ev[pp])
                for half in range(2):
                    for kt in range(NKT):
                        n = p.do("T", lambda e, kt=kt, half=half, pp=pp, b=b, wbs=wbs:
                                 e.matmul(
                                     [pAh, pBh][pp][half],
                                     wq_sb[wbs[half]][:, kt * 128:(kt + 1) * 128],
                                     xtm_sb[b][:, kt * 512:(kt + 1) * 512],
                                     start=(kt == 0), stop=(kt == NKT - 1)),
                                 t3 if kt == NKT - 1 else None)
                        if kt == NKT - 1:
                            mm_done = n
                xtm_cons[b] = mm_done
                wq_cons[wbs[0]] = mm_done
                wq_cons[wbs[1]] = mm_done
                qb = hp % 2
                PA = [pAh, pBh][pp][0]
                PB = [pAh, pBh][pp][1]
                p.w("V", t3, mm_done)
                p.w("V", q3, tabd)
                p.w("V", g3, qo_st[qb])
                p.do("V", lambda e, hp=hp, PA=PA, b=b: e.scalar_tensor_tensor(
                    qtmp1[:], PA, bqtA[:, hp:hp + 1], tqa_sb[b][:], ALU.add, ALU.mult))
                p.do("V", lambda e, hp=hp, PB=PB, b=b: e.scalar_tensor_tensor(
                    qtmp2[:], PB, bqtB[:, hp:hp + 1], tqb_sb[b][:], ALU.add, ALU.mult))
                p.do("V", lambda e, qb=qb: e.tensor_sub(qlo_sb[qb][:], qtmp1[:], qtmp2[:]))
                p.do("V", lambda e, hp=hp, PA=PA, b=b: e.scalar_tensor_tensor(
                    qtmp1[:], PA, bqtA[:, hp:hp + 1], tqb_sb[b][:], ALU.add, ALU.mult))
                p.do("V", lambda e, hp=hp, PB=PB, b=b: e.scalar_tensor_tensor(
                    qtmp2[:], PB, bqtB[:, hp:hp + 1], tqa_sb[b][:], ALU.add, ALU.mult))
                bank_ev[pp] = p.do("V", lambda e, qb=qb: e.tensor_add(
                    qhi_sb[qb][:], qtmp1[:], qtmp2[:]), v3)
                evd = bank_ev[pp]
                p.w("S", v3, evd)
                p.do("S", lambda e, qb=qb: e.activation(qsqlo[qb][:], qlo_sb[qb][:], AF.Square))
                sqd = p.do("S", lambda e, qb=qb: e.activation(
                    qsqhi[qb][:], qhi_sb[qb][:], AF.Square), s3)
                nb = [pD, pE][hp % 2]
                p.w("T", s3, sqd)
                p.w("T", v3, norm_ev[hp % 2])
                p.do("T", lambda e, qb=qb, nb=nb: e.matmul(
                    nb[0:1, :], onescol[:], qsqlo[qb][:], start=True, stop=False,
                    skip_group_check=True))
                od = p.do("T", lambda e, qb=qb, nb=nb: e.matmul(
                    nb[0:1, :], onescol[:], qsqhi[qb][:], start=False, stop=True,
                    skip_group_check=True), t3)
                p.w("V", t3, od)
                norm_ev[hp % 2] = p.do("V", lambda e, nb=nb, sc=sc, hp=hp: e.reduce_max(
                    qmaxrow[:, sc * 16 + hp:sc * 16 + hp + 1], nb[0:1, :], axis=AX.X), v3)
                p.w("G", v3, evd)
                for (buf, hoff) in ((qlo_sb, 0), (qhi_sb, 64)):
                    for two in range(2):
                        hh = 2 * hp + two
                        qo_st[qb] = 16 + g3.n
                        dma("G", qTs[hh * 128 + hoff:hh * 128 + hoff + 64,
                                    sc * 512:(sc + 1) * 512],
                            buf[qb][two * 64:(two + 1) * 64, :], g3)
        p2_v_final = v3.n
        p2_g_final = g3.n

    if upto >= 5:
        q6 = p.sem("q6")
        v6 = p.sem("v6")
        g6 = p.sem("g6")
        d6 = p.sem("d6")
        s6b = p.sem("s6b")
        wf_sb = [_sb_at(nc, f"wfsb{i}", [128, 4096], F32, 96 * KB + i * 16 * KB)
                 for i in range(2)]
        qf_sb = _sb_at(nc, "qf_sb", [128, 4096], BF16, 128 * KB)
        btmp6 = _sb_at(nc, "btmp6", [128, 4096], BF16, 136 * KB)
        strip6 = [_sb_at(nc, f"strip6{i}", [128, 4096], BF16, 144 * KB + i * 8 * KB)
                  for i in range(2)]
        absb6 = _sb_at(nc, "absb6", [128, 4096], F32, 160 * KB)
        wf_cons = [0, 0]
        qf_cons = [0]
        strip_cons = [0, 0]

        def p6a(ot):
            wb = ot % 2

            def emit():
                p.w("Q", v6, wf_cons[wb])
                dma("Q", wf_sb[wb][:], Wf[ot * 128:(ot + 1) * 128, :], q6)
                ld = q6.n
                p.w("V", q6, ld)
                p.w("V", g6, qf_cons[0])
                p.do("V", lambda e, wb=wb: e.tensor_scalar_mul(
                    absb6[:], wf_sb[wb][:], -1.0))
                mx = p.do("V", lambda e, wb=wb: e.tensor_max(
                    absb6[:], absb6[:], wf_sb[wb][:]), v6)
                p.w("V", v6, mx)
                rd = p.do("V", lambda e: e.tensor_reduce(
                    stt1[:, 0:1], absb6[:], AX.X, ALU.add), v6)
                p.w("V", v6, rd)
                KSC = 4096.0 / 0.7
                sum_b = stt1[:, 0:1].broadcast_to([128, 4096])
                p.do("V", lambda e, wb=wb, sum_b=sum_b: e.scalar_tensor_tensor(
                    qf_sb[:], wf_sb[wb][:], KSC, sum_b, ALU.mult, ALU.is_gt))
                p.do("V", lambda e, wb=wb, sum_b=sum_b: e.scalar_tensor_tensor(
                    btmp6[:], wf_sb[wb][:], -KSC, sum_b, ALU.mult, ALU.is_gt))
                wf_cons[wb] = p.do("V", lambda e: e.tensor_sub(qf_sb[:], qf_sb[:], btmp6[:]), v6)
                p.w("G", v6, wf_cons[wb])
                qf_cons[0] = 16 + g6.n
                dma("G", qrow[ot * 128:(ot + 1) * 128, :], qf_sb[:], g6)
            return emit

        def p6b(kt):
            sb_ = kt % 2

            def emit():
                p.w("Q", g6, 32 * 16)
                p.w("Q", s6b, strip_cons[sb_])
                p.do("Q", lambda e, kt=kt, sb_=sb_: e.dma_start_transpose(
                    strip6[sb_][:], qrow[:, kt * 128:(kt + 1) * 128]), d6, 16)
                trd = d6.n
                p.w("G", d6, trd)
                strip_cons[sb_] = 16 + s6b.n
                dma("G", qWT[kt * 128:(kt + 1) * 128, :], strip6[sb_][:], s6b)
            return emit

    if upto >= 4:
        p.barrier()
        t4 = p.sem("t4")
        v4 = p.sem("v4")
        s4 = p.sem("s4")
        q4 = p.sem("q4")
        g4 = p.sem("g4")
        qc_sb = _sb_at(nc, "qc_sb", [128, H * 512], BF16, BASE)
        kg_sb = [_sb_at(nc, f"kg{i}", [128, S], BF16, 56 * KB + i * 8 * KB) for i in range(2)]
        vg_sb = [_sb_at(nc, f"vg{i}", [128, S], BF16, 72 * KB + i * 8 * KB) for i in range(2)]
        pt_sb = [_sb_at(nc, f"pt{i}", [128, 1024], BF16, 88 * KB + i * 2 * KB) for i in range(2)]
        recb = _sb_at(nc, "recb", [128, 512], F32, 92 * KB)
        ctx_sb = [_sb_at(nc, f"ctx{i}", [128, 512], BF16, 94 * KB + i * KB) for i in range(2)]

        p.w("V", v3, p2_v_final)
        p.w("V", v1, p1a_km_final)
        vnu = p.do("V", lambda e: e.tensor_scalar(
            rtmp1, qmaxrow, km2, None, ALU.mult), v4)
        p.w("V", v4, vnu)
        first = True
        for j in (0, 2, 4, 6, 8, 10):
            c = float(2 ** j)
            n_ = p.do("V", lambda e, c=c: e.tensor_scalar(
                rtmp2, rtmp1, 0.5 / c, 0.5 * c, ALU.mult, ALU.add), v4)
            p.w("V", v4, n_)
            if first:
                n_ = p.do("V", lambda e: e.tensor_copy(negu_row, rtmp2), v4)
                first = False
            else:
                n_ = p.do("V", lambda e: e.tensor_tensor(
                    negu_row, negu_row, rtmp2, ALU.min), v4)
            p.w("V", v4, n_)
        nr = p.do("V", lambda e: e.tensor_scalar(
            negu_row, negu_row, -1.01, -1.0, ALU.mult, ALU.add), v4)
        p.w("V", v4, nr)
        vnu2 = p.do("V", lambda e: e.tensor_copy(negu_bf[:], negu_row), v4)
        p.w("T", v4, vnu2)
        tnu = p.do("T", lambda e: e.matmul(
            pF[:, 0:64], onesrow_bf[:], negu_bf[:], start=True, stop=True), t4)
        p.w("S", t4, tnu)
        negub_done = p.do("S", lambda e: e.copy(negUb[:], pF[:, 0:64]), s4)

        p6a_list = [p6a(ot) for ot in range(32)] if upto >= 5 else []
        p6b_list = [p6b(kt) for kt in range(NKT)] if upto >= 5 else []
        p6a_i = 0
        p6b_i = 0

        qc_cons = 0
        kg_cons = [0, 0]
        vg_cons = [0, 0]
        pt_cons = [0, 0]
        pt_mask_req = [None, None]
        sc_ev = [0, 0]
        ctx_ev = 0
        sum_ev = 0
        rec_ev = 0
        ctx_st = [0, 0]
        qrow_targets = []

        for c in range(4):
            Tc = (8 * c + 8) * 128
            npair = 4 * c + 4
            p.w("Q", t4, qc_cons)
            p.w("Q", g3 if upto >= 3 else gI, p2_g_final if upto >= 3 else 0)
            dma("Q", qc_sb[:].rearrange("p (h s) -> p h s", s=512),
                qTs.rearrange("(h p) s -> p h s", p=128)[:, :, c * 512:(c + 1) * 512], q4)
            qc_ld = q4.n
            for g in range(G):
                gb = g % 2
                p.w("Q", t4, kg_cons[gb])
                p.w("Q", g1 if upto >= 1 else gI, p1a_g_final if upto >= 1 else 0)
                dma("Q", kg_sb[gb][:, 0:Tc], kT[g * 128:(g + 1) * 128, 0:Tc], q4)
                p.w("Q", t4, vg_cons[gb])
                p.w("Q", g2 if upto >= 2 else gI, p1b_g_final if upto >= 2 else 0)
                dma("Q", vg_sb[gb][:].rearrange("p (tt d) -> p tt d", d=128)[:, 0:Tc // 128, :],
                    v2[0:Tc, g * 128:(g + 1) * 128].rearrange("(tt t) d -> t tt d", t=128),
                    q4)
                kv_ld = q4.n
                for r in range(4):
                    h = g * 4 + r
                    for pr in range(npair):
                        l0 = max(0, pr - 4 * c)
                        s0 = 128 * l0
                        W = 512 - s0
                        pb = pr % 2
                        p.w("T", q4, max(qc_ld, kv_ld))
                        p.w("T", s4, sc_ev[pb])
                        p.w("T", s4, negub_done)
                        p.do("T", lambda e, pr=pr, s0=s0, gb=gb, h=h, pb=pb: e.matmul(
                            [pA, pB][pb][:, s0:512],
                            kg_sb[gb][:, (2 * pr) * 128:(2 * pr + 1) * 128],
                            qc_sb[:, h * 512 + s0:h * 512 + 512],
                            start=True, stop=True))
                        scd = p.do("T", lambda e, pr=pr, s0=s0, gb=gb, h=h, pb=pb: e.matmul(
                            [pA, pB][pb][:, 512 + s0:1024],
                            kg_sb[gb][:, (2 * pr + 1) * 128:(2 * pr + 2) * 128],
                            qc_sb[:, h * 512 + s0:h * 512 + 512],
                            start=True, stop=True), t4)
                        kg_cons[gb] = scd
                        qc_cons = scd
                        p.w("S", t4, scd)
                        if pt_cons[pb]:
                            p.w("S", t4, pt_cons[pb])
                        if pt_mask_req[pb] is not None:
                            p.w("S", v4, pt_mask_req[pb])
                        expd = p.do("S", lambda e, pb=pb, s0=s0, c=c, h=h, W=W: e.activation(
                            pt_sb[pb][:].rearrange("p (two x) -> p two x", two=2)[:, :, s0:512],
                            [pA, pB][pb][:].rearrange("p (two x) -> p two x", two=2)[:, :, s0:512],
                            AF.Exp, bias=negUb[:, c * 16 + h // 2:c * 16 + h // 2 + 1], scale=1.0),
                            s4)
                        sc_ev[pb] = expd
                        masked = (4 * c <= pr <= 4 * c + 3)
                        if masked:
                            l = pr - 4 * c
                            p.w("V", s4, expd)
                            p.do("V", lambda e, pb=pb, l=l, pr=pr: e.tensor_mul(
                                pt_sb[pb][:, l * 128:(l + 1) * 128],
                                pt_sb[pb][:, l * 128:(l + 1) * 128],
                                dmask[:, (pr * 2) * 128:(pr * 2 + 1) * 128]))
                            mkd = p.do("V", lambda e, pb=pb, l=l, pr=pr: e.tensor_mul(
                                pt_sb[pb][:, 512 + l * 128:512 + (l + 1) * 128],
                                pt_sb[pb][:, 512 + l * 128:512 + (l + 1) * 128],
                                dmask[:, (pr * 2 + 1) * 128:(pr * 2 + 2) * 128]), v4)
                            pt_mask_req[pb] = mkd
                        else:
                            pt_mask_req[pb] = None
                        p.w("T", s4, expd)
                        if masked:
                            p.w("T", v4, mkd)
                        if pr == 0:
                            p.w("T", v4, ctx_ev)
                            p.w("T", v4, sum_ev)
                        p.do("T", lambda e, pr=pr, s0=s0, gb=gb, pb=pb: e.matmul(
                            pC[:, s0:512], vg_sb[gb][:, (2 * pr) * 128:(2 * pr + 1) * 128],
                            pt_sb[pb][:, s0:512], start=(pr == 0), stop=False,
                            skip_group_check=True))
                        pvd = p.do("T", lambda e, pr=pr, s0=s0, gb=gb, pb=pb: e.matmul(
                            pC[:, s0:512], vg_sb[gb][:, (2 * pr + 1) * 128:(2 * pr + 2) * 128],
                            pt_sb[pb][:, 512 + s0:1024], start=False,
                            stop=(pr == npair - 1), skip_group_check=True), t4)
                        vg_cons[gb] = pvd
                        p.do("T", lambda e, pr=pr, s0=s0, pb=pb: e.matmul(
                            pD[0:1, s0:512], onescol[:], pt_sb[pb][:, s0:512],
                            start=(pr == 0), stop=False, skip_group_check=True))
                        oned = p.do("T", lambda e, pr=pr, s0=s0, pb=pb: e.matmul(
                            pD[0:1, s0:512], onescol[:], pt_sb[pb][:, 512 + s0:1024],
                            start=False, stop=(pr == npair - 1),
                            skip_group_check=True), t4)
                        pt_cons[pb] = oned
                    p.w("V", t4, pt_cons[0])
                    p.w("V", t4, pt_cons[1])
                    sum_ev = p.do("V", lambda e: e.reciprocal(rec_row, pD[0:1, :]), v4)
                    p.w("T", v4, max(sum_ev, rec_ev))
                    bcd = p.do("T", lambda e: e.matmul(
                        pE[:], onesrow_f[:], rec_row, start=True, stop=True), t4)
                    cb = (g * 4 + r) % 2
                    p.w("V", t4, bcd)
                    p.w("V", g4, ctx_st[cb])
                    rec_ev = p.do("V", lambda e: e.tensor_copy(recb[:], pE[:]), v4)
                    ctx_ev = p.do("V", lambda e, cb=cb: e.tensor_mul(
                        ctx_sb[cb][:], pC[:], recb[:]), v4)
                    p.w("G", v4, ctx_ev)
                    ctx_st[cb] = 16 + g4.n
                    dma("G", ctxT[h * 128:(h + 1) * 128, c * 512:(c + 1) * 512],
                        ctx_sb[cb][:], g4)
                it = c * 8 + g
                if upto >= 5:
                    if c < 3:
                        want = min(32, round((it + 1) * 32 / 24))
                        while p6a_i < want:
                            qrow_targets.append(p6a_list[p6a_i]())
                            p6a_i += 1
                    else:
                        while p6a_i < 32:
                            qrow_targets.append(p6a_list[p6a_i]())
                            p6a_i += 1
                        want_b = min(NKT, (g + 1) * 4)
                        while p6b_i < want_b:
                            p6b_list[p6b_i]()
                            p6b_i += 1
        p3_g_final = g4.n
        if upto >= 5:
            p6_d_final = s6b.n

    if upto >= 6:
        p.barrier()
        t5 = p.sem("t5")
        v5 = p.sem("v5")
        s5 = p.sem("s5")
        q5 = p.sem("q5")
        g5 = p.sem("g5")
        ctxh_sb = _sb_at(nc, "ctxh", [128, H * 1024], BF16, BASE)
        wo_sb = [_sb_at(nc, f"wosb{i}", [128, NKT * 512], BF16, 88 * KB + i * 32 * KB)
                 for i in range(2)]
        bo_sl = [_sb_at(nc, f"bosl{i}", [128, 512], F32, 152 * KB + i * 2 * KB) for i in range(2)]
        xm_sl = [_sb_at(nc, f"xmsl{i}", [128, 512], F32, 156 * KB + i * 2 * KB) for i in range(2)]
        tmpf4 = _sb_at(nc, "tmpf4", [128, 512], F32, 160 * KB)
        x1c_sb = [_sb_at(nc, f"x1c{i}", [128, 512], F32, 162 * KB + i * 2 * KB) for i in range(2)]
        junkb = _sb_at(nc, "junkb4", [128, 4096], BF16, 166 * KB)
        xp_full = _sb_at(nc, "xp4", [128, 4096], F32, 174 * KB)
        xr_full = _sb_at(nc, "xr4", [128, 4096], BF16, 190 * KB)
        for nch in range(8):
            dma("G", wo_bf[nch * 128:(nch + 1) * 128, :].rearrange(
                    "p (hd n) -> p hd n", n=512),
                Wo.rearrange("(hd p) (nch n) -> nch p hd n", p=128, n=512)[nch], gI)
        woc_done = gI.n

        ctxh_cons = 0
        wo_cons = [0, 0]
        bank_ev4 = [0, 0, 0, 0]
        bo_cons = [0, 0]
        xm_cons = [0, 0]
        x1c_st = [0, 0]
        xp_cons = 0
        xr_st = 0
        for grp in range(2):
            p.w("Q", t5, ctxh_cons)
            p.w("Q", g4, p3_g_final)
            dma("Q", ctxh_sb[:].rearrange("p (h s) -> p h s", s=1024),
                ctxT.rearrange("(h p) s -> p h s", p=128)[:, :, grp * 1024:(grp + 1) * 1024],
                q5)
            ctxh_ld = q5.n
            for nch in range(8):
                wb = nch % 2
                p.w("Q", gI, woc_done)
                p.w("Q", t5, wo_cons[wb])
                dma("Q", wo_sb[wb][:], wo_bf[nch * 128:(nch + 1) * 128, :], q5)
                wold = q5.n
                p.w("G", v5, bo_cons[wb])
                dma("G", bo_sl[wb][:], bass.AP(
                    bo_i.tensor, nch * 512, [[0, 128], [1, 512]]), g5)
                bold = g5.n
                for st8 in range(8):
                    st = grp * 8 + st8
                    i4 = (nch * 8 + st8) % 4
                    ldb = (nch * 8 + st8) % 2
                    p.w("Q", q5, 0)
                    p.w("Q", v5, xm_cons[ldb])
                    dma("Q", xm_sl[ldb][:],
                        x_mine[st * 128:(st + 1) * 128, nch * 512:(nch + 1) * 512], q5)
                    xmld = q5.n
                    p.w("T", q5, max(ctxh_ld, wold))
                    p.w("T", v5, bank_ev4[i4])
                    for hd in range(32):
                        n = p.do("T", lambda e, hd=hd, st8=st8, wb=wb, i4=i4: e.matmul(
                            p4rot[i4],
                            ctxh_sb[:, hd * 1024 + st8 * 128:hd * 1024 + (st8 + 1) * 128],
                            wo_sb[wb][:, hd * 512:(hd + 1) * 512],
                            start=(hd == 0), stop=(hd == 31)),
                            t5 if hd == 31 else None)
                        if hd == 31:
                            mmd = n
                    ctxh_cons = mmd
                    wo_cons[wb] = mmd
                    xc = (nch * 8 + st8) % 2
                    p.w("V", t5, mmd)
                    p.w("V", g5, bold)
                    p.w("V", q5, xmld)
                    p.w("V", g5, x1c_st[xc])
                    p.do("V", lambda e, i4=i4, wb=wb: e.tensor_add(
                        tmpf4[:], p4rot[i4], bo_sl[wb][:]))
                    ev = p.do("V", lambda e, ldb=ldb, xc=xc:
                              e.scalar_tensor_tensor(
                                  x1c_sb[xc][:], tmpf4[:], 0.0, xm_sl[ldb][:],
                                  ALU.add, ALU.add),
                              v5)
                    bank_ev4[i4] = ev
                    xm_cons[ldb] = ev
                    bo_cons[wb] = ev
                    p.w("G", v5, ev)
                    x1c_st[xc] = 16 + g5.n
                    dma("G", x1pre[st * 128:(st + 1) * 128, nch * 512:(nch + 1) * 512],
                        x1c_sb[xc][:], g5)
            grp_done_v = v5.n
            grp_done_s = s5.n
            grp_done_g = g5.n
            p.w("V", v5, grp_done_v)
            for st8 in range(8):
                st = grp * 8 + st8
                p.w("Q", v5, xp_cons)
                p.w("Q", g5, grp_done_g)
                dma("Q", xp_full[:], x1pre[st * 128:(st + 1) * 128, :], q5)
                xld = q5.n
                p.w("V", q5, xld)
                n_ = p.do("V", lambda e: e.tensor_reduce(
                    stt1[:, 0:1], xp_full[:], AX.X, ALU.add), v5)
                p.w("V", v5, n_)
                n_ = p.do("V", lambda e: e.scalar_tensor_tensor(
                    junkb[:], xp_full[:], 1.0, xp_full[:], ALU.mult, ALU.mult,
                    accum_out=stt1[:, 1:2]), v5)
                p.w("V", v5, n_)
                n_ = p.do("V", lambda e: e.tensor_scalar_mul(
                    stt1[:, 2:3], stt1[:, 0:1], 1.0 / 4096.0), v5)
                p.w("V", v5, n_)
                n_ = p.do("V", lambda e: e.tensor_scalar_mul(
                    stt1[:, 3:4], stt1[:, 1:2], 1.0 / 4096.0), v5)
                p.w("V", v5, n_)
                n_ = p.do("V", lambda e: e.tensor_mul(
                    stt2[:, 0:1], stt1[:, 2:3], stt1[:, 2:3]), v5)
                p.w("V", v5, n_)
                n_ = p.do("V", lambda e: e.tensor_sub(
                    stt2[:, 1:2], stt1[:, 3:4], stt2[:, 0:1]), v5)
                p.w("V", v5, n_)
                n_ = p.do("V", lambda e: e.tensor_scalar_add(
                    stt2[:, 1:2], stt2[:, 1:2], EPS), v5)
                p.w("V", v5, n_)
                n_ = p.do("V", lambda e: e.reciprocal(stt2[:, 2:3], stt2[:, 1:2]), v5)
                p.w("V", v5, n_)
                n_ = p.do("V", lambda e: e.tensor_scalar_min(
                    stt2[:, 3:4], stt2[:, 2:3], 1.0), v5)
                p.w("V", v5, n_)
                for _it in range(16):
                    n_ = p.do("V", lambda e: e.tensor_mul(
                        stt3[:, 0:1], stt2[:, 3:4], stt2[:, 3:4]), v5)
                    p.w("V", v5, n_)
                    n_ = p.do("V", lambda e: e.tensor_mul(
                        stt3[:, 0:1], stt3[:, 0:1], stt2[:, 1:2]), v5)
                    p.w("V", v5, n_)
                    n_ = p.do("V", lambda e: e.tensor_scalar(
                        stt3[:, 0:1], stt3[:, 0:1], -0.5, 1.5, ALU.mult, ALU.add), v5)
                    p.w("V", v5, n_)
                    n_ = p.do("V", lambda e: e.tensor_mul(
                        stt2[:, 3:4], stt2[:, 3:4], stt3[:, 0:1]), v5)
                    p.w("V", v5, n_)
                p.w("V", g5, xr_st)
                xp_cons = p.do("V", lambda e: e.tensor_scalar(
                    xr_full[:], xp_full[:], stt1[:, 2:3], stt2[:, 3:4],
                    ALU.subtract, ALU.mult), v5)
                p.w("G", v5, xp_cons)
                xr_st = 16 + g5.n
                dma("G", x1row[st * 128:(st + 1) * 128, :], xr_full[:],
                    g5)
        p45_g_final = g5.n

    if upto >= 7:
        p.barrier()
        t7 = p.sem("t7")
        v7 = p.sem("v7")
        s7 = p.sem("s7")
        q7 = p.sem("q7")
        g7 = p.sem("g7")
        q8 = p.sem("q8")
        x1T_sb = _sb_at(nc, "x1T", [128, NKT * 1024], BF16, BASE)
        qw_sb = [_sb_at(nc, f"qwsb{i}", [128, NKT * 512], BF16, 88 * KB + i * 32 * KB)
                 for i in range(2)]
        fb_sl = [_sb_at(nc, f"fbsl{i}", [128, 512], F32, 152 * KB + i * 2 * KB) for i in range(2)]
        x1s_sl = [_sb_at(nc, f"x1ssl{i}", [128, 512], BF16, 156 * KB + i * KB) for i in range(2)]
        tmpf7 = _sb_at(nc, "tmpf7", [128, 512], F32, 158 * KB)
        x2c_sb = [_sb_at(nc, f"x2c{i}", [128, 512], F32, 160 * KB + i * 2 * KB) for i in range(2)]
        junkb7 = _sb_at(nc, "junkb7", [128, 4096], BF16, 164 * KB)
        xp_full7 = _sb_at(nc, "xp7", [128, 4096], F32, 172 * KB)
        xr_full7 = _sb_at(nc, "xr7", [128, 4096], F32, 188 * KB)

        x1t_cons = 0
        qw_cons = [0, 0]
        bank_ev7 = [0, 0, 0, 0]
        fb_cons = [0, 0]
        x1s_cons = [0, 0]
        x2c_st = [0, 0]
        xp_cons = 0
        xr_st = 0
        for grp in range(2):
            for kt in range(NKT):
                eng = "Q"
                p.w(eng, g5, p45_g_final)
                if grp == 1:
                    p.w(eng, t7, x1t_cons)
                p.do(eng, lambda e, kt=kt, grp=grp: e.dma_start_transpose(
                    x1T_sb[:, kt * 1024:(kt + 1) * 1024],
                    x1row[grp * 1024:(grp + 1) * 1024, kt * 128:(kt + 1) * 128]), q7, 16)
            x1trd = q7.n
            for och in range(8):
                wb = och % 2
                p.w("Q", s6b, p6_d_final)
                p.w("Q", t7, qw_cons[wb])
                dma("Q", qw_sb[wb][:].rearrange("p (kt n) -> p kt n", n=512),
                    qWT.rearrange("(kt p) o -> p kt o", p=128)[:, :, och * 512:(och + 1) * 512],
                    q7)
                qwld = q7.n
                p.w("G", v7, fb_cons[wb])
                dma("G", fb_sl[wb][:], bass.AP(
                    fb_i.tensor, och * 512, [[0, 128], [1, 512]]), g7)
                fbld = g7.n
                for st8 in range(8):
                    st = grp * 8 + st8
                    i7 = (och * 8 + st8) % 4
                    ldb = (och * 8 + st8) % 2
                    p.w("Q", v7, x1s_cons[ldb])
                    dma("Q", x1s_sl[ldb][:],
                        x1row[st * 128:(st + 1) * 128, och * 512:(och + 1) * 512], q7)
                    x1sld = q7.n
                    p.w("T", q7, max(x1trd, qwld))
                    p.w("T", v7, bank_ev7[i7])
                    for kt in range(NKT):
                        n = p.do("T", lambda e, kt=kt, st8=st8, wb=wb, i7=i7: e.matmul(
                            p4rot[i7],
                            x1T_sb[:, kt * 1024 + st8 * 128:kt * 1024 + (st8 + 1) * 128],
                            qw_sb[wb][:, kt * 512:(kt + 1) * 512],
                            start=(kt == 0), stop=(kt == NKT - 1)),
                            t7 if kt == NKT - 1 else None)
                        if kt == NKT - 1:
                            mmd = n
                    x1t_cons = mmd
                    qw_cons[wb] = mmd
                    xc = (och * 8 + st8) % 2
                    p.w("V", t7, mmd)
                    p.w("V", g7, fbld)
                    p.w("V", q7, x1sld)
                    p.w("V", g7, x2c_st[xc])
                    p.do("V", lambda e, i7=i7, wb=wb: e.tensor_add(
                        tmpf7[:], p4rot[i7], fb_sl[wb][:]))
                    ev = p.do("V", lambda e, ldb=ldb, xc=xc:
                              e.scalar_tensor_tensor(
                                  x2c_sb[xc][:], tmpf7[:], 0.0, x1s_sl[ldb][:],
                                  ALU.add, ALU.add),
                              v7)
                    bank_ev7[i7] = ev
                    x1s_cons[ldb] = ev
                    fb_cons[wb] = ev
                    p.w("G", v7, ev)
                    x2c_st[xc] = 16 + g7.n
                    dma("G", x2pre[st * 128:(st + 1) * 128, och * 512:(och + 1) * 512],
                        x2c_sb[xc][:], g7)
            grp_done_v = v7.n
            grp_done_s = s7.n
            grp_done_g = g7.n
            p.w("V", v7, grp_done_v)
            for st8 in range(8):
                st = grp * 8 + st8
                p.w("Q", v7, xp_cons)
                p.w("Q", g7, grp_done_g)
                dma("Q", xp_full7[:], x2pre[st * 128:(st + 1) * 128, :], q7)
                xld = q7.n
                p.w("V", q7, xld)
                n_ = p.do("V", lambda e: e.tensor_reduce(
                    stt1[:, 0:1], xp_full7[:], AX.X, ALU.add), v7)
                p.w("V", v7, n_)
                n_ = p.do("V", lambda e: e.scalar_tensor_tensor(
                    junkb7[:], xp_full7[:], 1.0, xp_full7[:], ALU.mult, ALU.mult,
                    accum_out=stt1[:, 1:2]), v7)
                p.w("V", v7, n_)
                n_ = p.do("V", lambda e: e.tensor_scalar_mul(
                    stt1[:, 2:3], stt1[:, 0:1], 1.0 / 4096.0), v7)
                p.w("V", v7, n_)
                n_ = p.do("V", lambda e: e.tensor_scalar_mul(
                    stt1[:, 3:4], stt1[:, 1:2], 1.0 / 4096.0), v7)
                p.w("V", v7, n_)
                n_ = p.do("V", lambda e: e.tensor_mul(
                    stt2[:, 0:1], stt1[:, 2:3], stt1[:, 2:3]), v7)
                p.w("V", v7, n_)
                n_ = p.do("V", lambda e: e.tensor_sub(
                    stt2[:, 1:2], stt1[:, 3:4], stt2[:, 0:1]), v7)
                p.w("V", v7, n_)
                n_ = p.do("V", lambda e: e.tensor_scalar_add(
                    stt2[:, 1:2], stt2[:, 1:2], EPS), v7)
                p.w("V", v7, n_)
                n_ = p.do("V", lambda e: e.reciprocal(stt2[:, 2:3], stt2[:, 1:2]), v7)
                p.w("V", v7, n_)
                n_ = p.do("V", lambda e: e.tensor_scalar_min(
                    stt2[:, 3:4], stt2[:, 2:3], 1.0), v7)
                p.w("V", v7, n_)
                for _it in range(16):
                    n_ = p.do("V", lambda e: e.tensor_mul(
                        stt3[:, 0:1], stt2[:, 3:4], stt2[:, 3:4]), v7)
                    p.w("V", v7, n_)
                    n_ = p.do("V", lambda e: e.tensor_mul(
                        stt3[:, 0:1], stt3[:, 0:1], stt2[:, 1:2]), v7)
                    p.w("V", v7, n_)
                    n_ = p.do("V", lambda e: e.tensor_scalar(
                        stt3[:, 0:1], stt3[:, 0:1], -0.5, 1.5, ALU.mult, ALU.add), v7)
                    p.w("V", v7, n_)
                    n_ = p.do("V", lambda e: e.tensor_mul(
                        stt2[:, 3:4], stt2[:, 3:4], stt3[:, 0:1]), v7)
                    p.w("V", v7, n_)
                p.w("V", q8, xr_st)
                xp_cons = p.do("V", lambda e: e.tensor_scalar(
                    xr_full7[:], xp_full7[:], stt1[:, 2:3], stt2[:, 3:4],
                    ALU.subtract, ALU.mult), v7)
                p.w("Q", v7, xp_cons)
                xr_st = 16 + q8.n
                dma("Q", out_c[st * 128:(st + 1) * 128, :], xr_full7[:],
                    q8)

    for s in p.sems:
        if s.n > 0:
            p.w("Q", s, s.n)

    with nc.Block() as block:
        @block.tensor
        def _(e):
            for f in p.st["T"]:
                f(e)

        @block.vector
        def _(e):
            for f in p.st["V"]:
                f(e)

        @block.scalar
        def _(e):
            for f in p.st["S"]:
                f(e)

        @block.sync
        def _(e):
            for f in p.st["Q"]:
                f(e)

        @block.gpsimd
        def _(e):
            for f in p.st["G"]:
                f(e)

    es.close()
    return nc


_BF = ml_dtypes.bfloat16


def _host_tables():
    inv = (THETA ** (-np.arange(64, dtype=np.float64) / 64.0))
    pos = np.arange(S, dtype=np.float64)
    ang = inv[:, None] * pos[None, :]
    cosk = np.cos(ang).astype(np.float32)
    sink = np.sin(ang).astype(np.float32)
    tabKA = np.concatenate([cosk, cosk], axis=0)
    tabKB = np.concatenate([sink, sink], axis=0)
    return tabKA, tabKB, cosk, sink


def _host_qtables(stripes, cosk, sink):
    cols = np.concatenate([np.arange(p * ST, (p + 1) * ST) for p in stripes])
    sc = np.float32(1.0 / np.sqrt(D))
    cq = cosk[:, cols] * sc
    sq = sink[:, cols] * sc
    return (np.concatenate([cq, cq], axis=0).astype(np.float32),
            np.concatenate([sq, sq], axis=0).astype(np.float32))


def _perm_cols(W, npair):
    r = W.reshape(W.shape[0], npair, 2, 2, 64)
    return np.ascontiguousarray(r.transpose(0, 1, 3, 2, 4).reshape(W.shape))


def _perm_bias(bvec, npair):
    b = np.asarray(bvec, np.float32).reshape(npair, 2, 2, 64)
    return (np.ascontiguousarray(b[:, :, 0, :].reshape(npair, 128)),
            np.ascontiguousarray(b[:, :, 1, :].reshape(npair, 128)))


def _host_masks(stripes):
    tri = np.triu(np.ones((128, 128), np.float32))
    m = np.zeros((MYST, 2, 128, 128), np.float32)
    for j, pos in enumerate(stripes):
        if pos == 2 * j + 1:
            m[j, 0] = 1.0
            m[j, 1] = tri
        else:
            m[j, 0] = tri
            m[j, 1] = 0.0
    return np.ascontiguousarray(m.transpose(2, 0, 1, 3).reshape(128, MYST * 2 * 128)
                                ).astype(_BF)


_CACHE = {}


def _get_nc():
    if "nc" not in _CACHE:
        _CACHE["nc"] = build_program()
    return _CACHE["nc"]


def make_in_maps(L0_attn_hidden_states, L0_attn_Wq, L0_attn_bq, L0_attn_Wk,
                 L0_attn_bk, L0_attn_Wv, L0_attn_bv, L0_attn_Wo, L0_attn_bo,
                 L0_ffn_weight, L0_ffn_bias):
    x = np.ascontiguousarray(np.asarray(L0_attn_hidden_states, dtype=np.float32))
    tabKA, tabKB, cosk, sink = _host_tables()
    bqA, bqB = _perm_bias(L0_attn_bq, 16)
    bkA, bkB = _perm_bias(L0_attn_bk, 4)
    shared = {
        "Wq": _perm_cols(np.asarray(L0_attn_Wq, np.float32), 16),
        "Wk": _perm_cols(np.asarray(L0_attn_Wk, np.float32), 4),
        "Wv": np.ascontiguousarray(np.asarray(L0_attn_Wv, np.float32)),
        "Wo": np.ascontiguousarray(np.asarray(L0_attn_Wo, np.float32)),
        "Wf": np.ascontiguousarray(np.asarray(L0_ffn_weight, np.float32)),
        "bqA_r": bqA, "bqB_r": bqB, "bkA_r": bkA, "bkB_r": bkB,
        "bv_i": np.asarray(L0_attn_bv, np.float32).reshape(1, -1),
        "bo_i": np.asarray(L0_attn_bo, np.float32).reshape(1, -1),
        "fb_i": np.asarray(L0_ffn_bias, np.float32).reshape(1, -1),
        "tabKA": tabKA, "tabKB": tabKB,
        "identf": np.eye(128, dtype=np.float32),
        "onescol_i": np.ones((128, 1), _BF),
        "onesrow_bf_i": np.ones((1, 128), _BF),
        "onesrow_f_i": np.ones((1, 128), np.float32),
    }
    sets = [SET_A, SET_B]
    qtabs = [_host_qtables(s, cosk, sink) for s in sets]
    masks = [_host_masks(s) for s in sets]
    in_maps = []
    for core in range(8):
        b = core // 2
        h = core % 2
        stripes = sets[h]
        xb = x[b]
        xm = np.ascontiguousarray(
            xb.reshape(NST, ST, HID)[stripes].reshape(MYST * ST, HID))
        m = dict(shared)
        m["x_full"] = xb
        m["x_mine"] = xm
        m["tabQA"], m["tabQB"] = qtabs[h]
        m["dmask_i"] = masks[h]
        in_maps.append(m)
    return in_maps


def kernel(**inputs):
    in_maps = make_in_maps(**inputs)
    nc = _get_nc()
    res = run_bass_kernel_spmd(nc, in_maps, core_ids=list(range(8)))
    out = np.empty((B, S, HID), dtype=np.float32)
    sets = [SET_A, SET_B]
    for core in range(8):
        b = core // 2
        h = core % 2
        stripes = sets[h]
        oc = np.asarray(res.results[core]["out_c"]).reshape(MYST, ST, HID)
        for j, pos in enumerate(stripes):
            out[b, pos * ST:(pos + 1) * ST, :] = oc[j]
    return out


# version 04
# speedup vs baseline: 1.1057x; 1.1057x over previous
import contextlib
import numpy as np
import ml_dtypes
import concourse.bass as bass
import concourse.mybir as mybir
from concourse.bass_utils import run_bass_kernel_spmd

F32 = mybir.dt.float32
BF16 = mybir.dt.bfloat16
AF = mybir.ActivationFunctionType
ALU = mybir.AluOpType
AX = mybir.AxisListType

B, S, HID = 4, 4096, 4096
H, G, D = 32, 8, 128
NST = 32
MYST = 16
ST = 128
NKT = 32
THETA = 10000.0
EPS = 1e-5

SET_A = sorted([4 * k for k in range(8)] + [4 * k + 3 for k in range(8)])
SET_B = sorted([4 * k + 1 for k in range(8)] + [4 * k + 2 for k in range(8)])
for _j, _p in enumerate(SET_A):
    assert _p in (2 * _j, 2 * _j + 1)
for _j, _p in enumerate(SET_B):
    assert _p in (2 * _j, 2 * _j + 1)

ENGS = "TVSQG"


class Sem:
    __slots__ = ("h", "n")

    def __init__(self, h):
        self.h = h
        self.n = 0


class Prog:

    def __init__(self, nc, es):
        self.nc = nc
        self.es = es
        self.st = {k: [] for k in ENGS}
        self.sems = []

    def sem(self, name):
        s = Sem(self.es.enter_context(self.nc.semaphore(name)))
        self.sems.append(s)
        return s

    def barrier(self):
        for eng in ENGS:
            for s in self.sems:
                if s.n > 0:
                    self.w(eng, s, s.n)

    def w(self, eng, sem, val):
        if val > 0:
            self.st[eng].append(lambda e, s=sem, v=val: e.wait_ge(s.h, v))

    def do(self, eng, fn, inc=None, amt=1):
        if inc is not None:
            inc.n += amt
        self.st[eng].append(
            lambda e, f=fn, i=inc, a=amt:
            (f(e).then_inc(i.h, a) if i is not None else f(e)))
        return inc.n if inc is not None else None


SB0 = 16384


def _sb_at(nc, name, shape, dtype, off_bytes):
    return nc.alloc_sbuf_tensor_at(name, shape, dtype, offset=SB0 + off_bytes)


def build_program(dbg=False, upto=99):
    nc = bass.Bass("TRN2", target_bir_lowering=False, debug=False)
    es = contextlib.ExitStack()
    p = Prog(nc, es)
    din = lambda n, sh, dt: nc.dram_tensor(n, sh, dt, kind="ExternalInput").ap()
    dout = lambda n, sh, dt: nc.dram_tensor(n, sh, dt, kind="ExternalOutput").ap()
    dscr = lambda n, sh, dt: nc.dram_tensor(
        n, sh, dt, kind="ExternalOutput" if dbg else "Internal").ap()

    x_full = din("x_full", [S, HID], F32)
    x_mine = din("x_mine", [MYST * ST, HID], F32)
    Wq = din("Wq", [HID, H * D], F32)
    Wk = din("Wk", [HID, G * D], F32)
    Wv = din("Wv", [HID, G * D], F32)
    Wo = din("Wo", [H * D, HID], F32)
    Wf = din("Wf", [HID, HID], F32)
    bqA_r = din("bqA_r", [16, 128], F32)
    bqB_r = din("bqB_r", [16, 128], F32)
    bkA_r = din("bkA_r", [4, 128], F32)
    bkB_r = din("bkB_r", [4, 128], F32)
    bv_i = din("bv_i", [1, G * D], F32)
    bo_i = din("bo_i", [1, HID], F32)
    fb_i = din("fb_i", [1, HID], F32)
    tabKA = din("tabKA", [128, S], F32)
    tabKB = din("tabKB", [128, S], F32)
    tabQA = din("tabQA", [128, MYST * ST], F32)
    tabQB = din("tabQB", [128, MYST * ST], F32)
    dmask_i = din("dmask_i", [128, MYST * 2 * 128], BF16)
    identf = din("identf", [128, 128], F32)
    onescol_i = din("onescol_i", [128, 1], BF16)
    onesrow_bf_i = din("onesrow_bf_i", [1, 128], BF16)
    onesrow_f_i = din("onesrow_f_i", [1, 128], F32)
    out_c = dout("out_c", [MYST * ST, HID], F32)

    xbf = dscr("xbf", [S, HID], BF16)
    xbfm = dscr("xbfm", [MYST * ST, HID], BF16)
    wq_bf = dscr("wq_bf", [H * D, HID], BF16)
    wo_bf = dscr("wo_bf", [8 * 128, NKT * 512], BF16)
    kT = dscr("kT", [G * 128, S], BF16)
    v2 = dscr("v2", [S, G * D], BF16)
    qTs = dscr("qTs", [H * 128, MYST * ST], BF16)
    ctxT = dscr("ctxT", [H * 128, MYST * ST], BF16)
    x1pre = dscr("x1pre", [MYST * ST, HID], F32)
    x1row = dscr("x1row", [MYST * ST, HID], BF16)
    qrow = dscr("qrow", [HID, HID], BF16)
    qWT = dscr("qWT", [NKT * 128, HID], BF16)
    x2pre = dscr("x2pre", [MYST * ST, HID], F32)

    KB = 1024
    cid = _sb_at(nc, "cid", [128, 128], F32, 0)
    onescol = _sb_at(nc, "onescol", [128, 1], BF16, 512)
    onesrow_bf = _sb_at(nc, "onesrow_bf", [1, 128], BF16, 544)
    onesrow_f = _sb_at(nc, "onesrow_f", [1, 128], F32, 800)
    rows = _sb_at(nc, "rows", [1, 2048], F32, 1312)
    bqtA = _sb_at(nc, "bqtA", [128, 16], F32, 9504)
    bqtB = _sb_at(nc, "bqtB", [128, 16], F32, 9568)
    bktA = _sb_at(nc, "bktA", [128, 4], F32, 9632)
    bktB = _sb_at(nc, "bktB", [128, 4], F32, 9664)
    negUb = _sb_at(nc, "negUb", [128, 64], F32, 9696)
    dmask = _sb_at(nc, "dmask", [128, MYST * 2 * 128], BF16, 10176)
    bvb = _sb_at(nc, "bvb", [128, G * D], F32, 18368)
    sums_all = _sb_at(nc, "sums_all", [128, 64], F32, 22464)
    sq_all = _sb_at(nc, "sq_all", [128, 64], F32, 22720)
    stt1 = _sb_at(nc, "stt1", [128, 8], F32, 22976)
    stt2 = _sb_at(nc, "stt2", [128, 8], F32, 23008)
    stt3 = _sb_at(nc, "stt3", [128, 8], F32, 23040)
    stt4 = _sb_at(nc, "stt4", [128, 8], F32, 23072)
    BASE = 24 * KB

    qmaxrow = rows[0:1, 0:64]
    km2 = rows[0:1, 128:129]
    rtmp1 = rows[0:1, 256:320]
    rtmp2 = rows[0:1, 384:448]
    negu_row = rows[0:1, 512:576]
    rec_row = rows[0:1, 1024:1536]
    negu_bf = _sb_at(nc, "negu_bf", [1, 64], BF16, 23104)
    kmx_t = rows[0:1, 129:130]

    pA = es.enter_context(nc.psum_tensor("pA", [128, 1024], F32))
    pB = es.enter_context(nc.psum_tensor("pB", [128, 1024], F32))
    pC2 = es.enter_context(nc.psum_tensor("pC2", [128, 1024], F32))
    pD = es.enter_context(nc.psum_tensor("pD", [128, 512], F32))
    pE = es.enter_context(nc.psum_tensor("pE", [128, 512], F32))
    pCh = [pC2[:, 0:512], pC2[:, 512:1024]]
    pAh = [pA[:, 0:512], pA[:, 512:1024]]
    pBh = [pB[:, 0:512], pB[:, 512:1024]]
    p4rot = [pA[:, 0:512], pA[:, 512:1024], pB[:, 0:512], pB[:, 512:1024]]

    gI = p.sem("gI")
    tI = p.sem("tI")
    vI = p.sem("vI")

    def dma(eng, dst, src, sem, **kw):
        return p.do(eng, lambda e, d=dst, s=src: e.dma_start(d, s, **kw),
                    sem, 16)

    dma("G", cid[:], identf, gI)
    dma("G", onescol[:], onescol_i, gI)
    dma("G", onesrow_bf[:], onesrow_bf_i, gI)
    dma("G", onesrow_f[:], onesrow_f_i, gI)
    dma("G", dmask[:], dmask_i, gI)
    btmp = _sb_at(nc, "btmp", [128, 128], F32, BASE)
    dma("G", btmp[0:16, :], bqA_r, gI)
    dma("G", btmp[16:32, :], bqB_r, gI)
    dma("G", btmp[32:36, :], bkA_r, gI)
    dma("G", btmp[36:40, :], bkB_r, gI)
    dma("G", bvb[:], bass.AP(bv_i.tensor, 0, [[0, 128], [1, G * D]]), gI)
    g_consts = gI.n
    p.w("V", gI, g_consts)
    p.do("V", lambda e: e.memset(rows[:], 0.0), vI)
    p.do("V", lambda e: e.memset(negUb[:], 0.0), vI)
    p.w("T", gI, g_consts)
    p.w("T", vI, vI.n)
    p.do("T", lambda e: e.transpose(pE[:, 0:128], btmp[:], cid[:]), tI)
    p.w("V", tI, tI.n)
    p.do("V", lambda e: e.tensor_copy(bqtA[:], pE[:, 0:16]), vI)
    p.do("V", lambda e: e.tensor_copy(bqtB[:], pE[:, 16:32]), vI)
    p.do("V", lambda e: e.tensor_copy(bktA[:], pE[:, 32:36]), vI)
    p.do("V", lambda e: e.tensor_copy(bktB[:], pE[:, 36:40]), vI)
    v_binit = vI.n

    p.w("G", tI, tI.n)
    wk_sb_g = _sb_at(nc, "wk_sb", [128, NKT * 1024], BF16, BASE)
    for h2 in range(2):
        dma("G", wk_sb_g[:].rearrange("p (kt n) -> p kt n", n=1024)[:, h2 * 16:(h2 + 1) * 16, :],
            Wk.rearrange("(kt p) n -> p kt n", p=128)[:, h2 * 16:(h2 + 1) * 16, :], gI)
    wk_done_g = gI.n
    xbf_done = []
    for tc in range(8):
        dma("G", xbf[tc * 512:(tc + 1) * 512, :], x_full[tc * 512:(tc + 1) * 512, :], gI)
        xbf_done.append(gI.n)
    xbfm_done = []
    for sc in range(4):
        dma("G", xbfm[sc * 512:(sc + 1) * 512, :], x_mine[sc * 512:(sc + 1) * 512, :], gI)
        xbfm_done.append(gI.n)
    wv_bf = dscr("wv_bf", [128, NKT * 1024], BF16)
    for h2 in range(2):
        dma("G", wv_bf.rearrange("p (kt n) -> p kt n", n=1024)[:, h2 * 16:(h2 + 1) * 16, :],
            Wv.rearrange("(kt p) n -> p kt n", p=128)[:, h2 * 16:(h2 + 1) * 16, :], gI)
    wv_done_g = gI.n
    for nt in range(32):
        dma("G", wq_bf[nt * 128:(nt + 1) * 128, :].rearrange(
                "p (kt n) -> p kt n", n=128),
            Wq.rearrange("(kt p) (nt n) -> nt p kt n", p=128, n=128)[nt], gI)
    wqc_done_g = gI.n
    for nch in range(8):
        dma("G", wo_bf[nch * 128:(nch + 1) * 128, :].rearrange(
                "p (hd n) -> p hd n", n=512),
            Wo.rearrange("(hd p) (nch n) -> nch p hd n", p=128, n=512)[nch], gI)
    woc_done_g = gI.n
    wk_cast0 = gI.n

    if upto >= 1:
        t1 = p.sem("t1")
        v1 = p.sem("v1")
        s1 = p.sem("s1")
        q1 = p.sem("q1")
        g1 = p.sem("g1")
        wk_sb = wk_sb_g
        xt_sb = [_sb_at(nc, f"xt{i}", [128, NKT * 512], BF16, 88 * KB + i * 32 * KB)
                 for i in range(2)]
        tka_sb = [_sb_at(nc, f"tka{i}", [128, 512], F32, 152 * KB + i * 2 * KB) for i in range(2)]
        tkb_sb = [_sb_at(nc, f"tkb{i}", [128, 512], F32, 156 * KB + i * 2 * KB) for i in range(2)]
        klo_sb = [_sb_at(nc, f"klo{i}", [128, 512], BF16, 160 * KB + i * KB) for i in range(2)]
        khi_sb = [_sb_at(nc, f"khi{i}", [128, 512], BF16, 162 * KB + i * KB) for i in range(2)]
        ksqlo = [_sb_at(nc, f"ksqlo{i}", [128, 512], BF16, 164 * KB + i * KB) for i in range(2)]
        ksqhi = [_sb_at(nc, f"ksqhi{i}", [128, 512], BF16, 166 * KB + i * KB) for i in range(2)]
        ktmp1 = _sb_at(nc, "ktmp1", [128, 512], F32, 168 * KB)
        ktmp2 = _sb_at(nc, "ktmp2", [128, 512], F32, 170 * KB)

        wk_done = wk_done_g

        xt_cons = [0, 0]
        bank_ev = [0, 0]
        norm_ev = [0, 0]
        ko_st = [0, 0]
        tr_done = []
        for tc in range(8):
            b = tc % 2
            for kt in range(NKT):
                eng = "Q"
                p.w(eng, gI, xbf_done[tc])
                if tc >= 2:
                    p.w(eng, t1, xt_cons[b])
                p.do(eng, lambda e, kt=kt, tc=tc, b=b: e.dma_start_transpose(
                    xt_sb[b][:, kt * 512:(kt + 1) * 512],
                    xbf[tc * 512:(tc + 1) * 512, kt * 128:(kt + 1) * 128]), q1, 16)
            tr_done.append(q1.n)
            dma("Q", tka_sb[b][:], tabKA[:, tc * 512:(tc + 1) * 512], q1)
            dma("Q", tkb_sb[b][:], tabKB[:, tc * 512:(tc + 1) * 512], q1)
            tab_done = q1.n
            for gp in range(4):
                pp = (tc * 4 + gp) % 2
                ph = [pAh, pBh][pp]
                kb = (tc * 4 + gp) % 2
                p.w("T", q1, tr_done[-1])
                p.w("T", gI, wk_done)
                p.w("T", v1, bank_ev[pp])
                for half in range(2):
                    nt2 = gp * 2 + half
                    for kt in range(NKT):
                        n = p.do("T", lambda e, nt2=nt2, kt=kt, half=half, pp=pp, b=b:
                                 e.matmul(
                                     ph[half] if False else [pAh, pBh][pp][half],
                                     wk_sb[:, kt * 1024 + nt2 * 128:kt * 1024 + (nt2 + 1) * 128],
                                     xt_sb[b][:, kt * 512:(kt + 1) * 512],
                                     start=(kt == 0), stop=(kt == NKT - 1)),
                                 t1 if kt == NKT - 1 else None)
                        if kt == NKT - 1:
                            mm_done = n
                xt_cons[b] = mm_done
                p.w("V", t1, mm_done)
                p.w("V", q1, tab_done)
                p.w("V", g1, ko_st[kb])
                PA = [pAh, pBh][pp][0]
                PB = [pAh, pBh][pp][1]
                p.do("V", lambda e, gp=gp, PA=PA, b=b: e.scalar_tensor_tensor(
                    ktmp1[:], PA, bktA[:, gp:gp + 1], tka_sb[b][:], ALU.add, ALU.mult))
                p.do("V", lambda e, gp=gp, PB=PB, b=b: e.scalar_tensor_tensor(
                    ktmp2[:], PB, bktB[:, gp:gp + 1], tkb_sb[b][:], ALU.add, ALU.mult))
                p.do("V", lambda e, kb=kb: e.tensor_sub(klo_sb[kb][:], ktmp1[:], ktmp2[:]))
                p.do("V", lambda e, gp=gp, PA=PA, b=b: e.scalar_tensor_tensor(
                    ktmp1[:], PA, bktA[:, gp:gp + 1], tkb_sb[b][:], ALU.add, ALU.mult))
                p.do("V", lambda e, gp=gp, PB=PB, b=b: e.scalar_tensor_tensor(
                    ktmp2[:], PB, bktB[:, gp:gp + 1], tka_sb[b][:], ALU.add, ALU.mult))
                bank_ev[pp] = p.do("V", lambda e, kb=kb: e.tensor_add(
                    khi_sb[kb][:], ktmp1[:], ktmp2[:]), v1)
                ev_done = bank_ev[pp]
                p.w("S", v1, ev_done)
                p.do("S", lambda e, kb=kb: e.activation(ksqlo[kb][:], klo_sb[kb][:], AF.Square))
                sq_done = p.do("S", lambda e, kb=kb: e.activation(
                    ksqhi[kb][:], khi_sb[kb][:], AF.Square), s1)
                nb = [pD, pE][gp % 2]
                p.w("T", s1, sq_done)
                p.w("T", v1, norm_ev[gp % 2])
                p.do("T", lambda e, kb=kb, nb=nb: e.matmul(
                    nb[0:1, :], onescol[:], ksqlo[kb][:], start=True, stop=False,
                    skip_group_check=True))
                ones_done = p.do("T", lambda e, kb=kb, nb=nb: e.matmul(
                    nb[0:1, :], onescol[:], ksqhi[kb][:], start=False, stop=True,
                    skip_group_check=True), t1)
                p.w("V", t1, ones_done)
                p.w("V", vI, v_binit)
                rmx = p.do("V", lambda e, nb=nb: e.reduce_max(kmx_t, nb[0:1, :], axis=AX.X), v1)
                p.w("V", v1, rmx)
                norm_ev[gp % 2] = p.do("V", lambda e: e.tensor_max(km2, km2, kmx_t), v1)
                p.w("V", v1, norm_ev[gp % 2])
                p.w("Q", v1, ev_done)
                for (buf, hoff) in ((klo_sb, 0), (khi_sb, 64)):
                    for two in range(2):
                        g = 2 * gp + two
                        ko_st[kb] = 16 + g1.n
                        dma("Q", kT[g * 128 + hoff:g * 128 + hoff + 64,
                                   tc * 512:(tc + 1) * 512],
                            buf[kb][two * 64:(two + 1) * 64, :], g1)
        p1a_g_final = g1.n
        p1a_km_final = v1.n

    if upto >= 2:
        p.barrier()
        t2 = p.sem("t2")
        v2s = p.sem("v2s")
        q2 = p.sem("q2")
        g2 = p.sem("g2")
        wv_sb = _sb_at(nc, "wv_sbp", [128, NKT * 1024], BF16, BASE)
        vbuf = [_sb_at(nc, f"vbuf{i}", [128, 512], BF16, 152 * KB + i * KB) for i in range(2)]
        p.w("Q", gI, wv_done_g)
        p.w("Q", t1, t1.n)
        dma("Q", wv_sb[:], wv_bf, q2)
        wv_done = q2.n
        xt_cons = [0, 0]
        bank_ev = [0, 0]
        vb_st = [0, 0]
        for tc in range(8):
            b = tc % 2
            for kt in range(NKT):
                eng = "Q"
                p.w(eng, gI, xbf_done[tc])
                if upto >= 1 and tc < 2:
                    p.w(eng, t1, t1.n)
                if tc >= 2:
                    p.w(eng, t2, xt_cons[b])
                p.do(eng, lambda e, kt=kt, tc=tc, b=b: e.dma_start_transpose(
                    xt_sb[b][:, kt * 512:(kt + 1) * 512],
                    xbf[tc * 512:(tc + 1) * 512, kt * 128:(kt + 1) * 128]), q2, 16)
            trd = q2.n
            for ts in range(4):
                for nch in range(2):
                    i2 = (tc * 8 + ts * 2 + nch) % 2
                    p.w("T", q2, max(trd, wv_done))
                    p.w("T", v2s, bank_ev[i2])
                    for kt in range(NKT):
                        n = p.do("T", lambda e, ts=ts, nch=nch, kt=kt, i2=i2, b=b: e.matmul(
                            pAh[i2],
                            xt_sb[b][:, kt * 512 + ts * 128:kt * 512 + (ts + 1) * 128],
                            wv_sb[:, kt * 1024 + nch * 512:kt * 1024 + (nch + 1) * 512],
                            start=(kt == 0), stop=(kt == NKT - 1)),
                            t2 if kt == NKT - 1 else None)
                        if kt == NKT - 1:
                            mm_done = n
                    xt_cons[b] = mm_done
                    p.w("V", t2, mm_done)
                    p.w("V", g2, vb_st[i2])
                    bank_ev[i2] = p.do("V", lambda e, nch=nch, i2=i2: e.tensor_add(
                        vbuf[i2][:], pAh[i2], bvb[:, nch * 512:(nch + 1) * 512]), v2s)
                    p.w("Q", v2s, bank_ev[i2])
                    vb_st[i2] = 16 + g2.n
                    dma("Q", v2[tc * 512 + ts * 128:tc * 512 + (ts + 1) * 128,
                               nch * 512:(nch + 1) * 512], vbuf[i2][:], g2)
        p1b_g_final = g2.n

    if upto >= 3:
        p.barrier()
        t3 = p.sem("t3")
        v3 = p.sem("v3")
        s3 = p.sem("s3")
        q3 = p.sem("q3")
        g3 = p.sem("g3")
        xtm_sb = [_sb_at(nc, f"xtm{i}", [128, NKT * 512], BF16, BASE + i * 32 * KB)
                  for i in range(2)]
        wq_sb = [_sb_at(nc, f"wqsb{i}", [128, NKT * 128], BF16, 88 * KB + i * 8 * KB)
                 for i in range(4)]
        tqa_sb = [_sb_at(nc, f"tqa{i}", [128, 512], F32, 120 * KB + i * 2 * KB) for i in range(2)]
        tqb_sb = [_sb_at(nc, f"tqb{i}", [128, 512], F32, 124 * KB + i * 2 * KB) for i in range(2)]
        qtmp1 = _sb_at(nc, "qtmp1", [128, 512], F32, 128 * KB)
        qtmp2 = _sb_at(nc, "qtmp2", [128, 512], F32, 130 * KB)
        qlo_sb = [_sb_at(nc, f"qlo{i}", [128, 512], BF16, 132 * KB + i * KB) for i in range(2)]
        qhi_sb = [_sb_at(nc, f"qhi{i}", [128, 512], BF16, 134 * KB + i * KB) for i in range(2)]
        qsqlo = [_sb_at(nc, f"qsqlo{i}", [128, 512], BF16, 136 * KB + i * KB) for i in range(2)]
        qsqhi = [_sb_at(nc, f"qsqhi{i}", [128, 512], BF16, 138 * KB + i * KB) for i in range(2)]
        wqc_done = wqc_done_g
        xtm_cons = [0, 0]
        wq_cons = [0, 0, 0, 0]
        bank_ev = [0, 0]
        norm_ev = [0, 0]
        qo_st = [0, 0]
        for sc in range(4):
            b = sc % 2
            for kt in range(NKT):
                eng = "Q"
                p.w(eng, gI, xbfm_done[sc])
                if upto >= 2 and sc < 2:
                    p.w(eng, t2, t2.n)
                if sc >= 2:
                    p.w(eng, t3, xtm_cons[b])
                p.do(eng, lambda e, kt=kt, sc=sc, b=b: e.dma_start_transpose(
                    xtm_sb[b][:, kt * 512:(kt + 1) * 512],
                    xbfm[sc * 512:(sc + 1) * 512, kt * 128:(kt + 1) * 128]), q3, 16)
            trd = q3.n
            dma("Q", tqa_sb[b][:], tabQA[:, sc * 512:(sc + 1) * 512], q3)
            dma("Q", tqb_sb[b][:], tabQB[:, sc * 512:(sc + 1) * 512], q3)
            tabd = q3.n
            for hp in range(16):
                wbs = [(hp % 2) * 2, (hp % 2) * 2 + 1]
                for half in range(2):
                    nt2 = 2 * hp + half
                    p.w("Q", gI, wqc_done)
                    p.w("Q", t3, wq_cons[wbs[half]])
                    dma("Q", wq_sb[wbs[half]][:], wq_bf[nt2 * 128:(nt2 + 1) * 128, :], q3)
                wqld = q3.n
                pp = hp % 2
                p.w("T", q3, max(trd, wqld))
                p.w("T", v3, bank_ev[pp])
                for half in range(2):
                    for kt in range(NKT):
                        n = p.do("T", lambda e, kt=kt, half=half, pp=pp, b=b, wbs=wbs:
                                 e.matmul(
                                     [pAh, pBh][pp][half],
                                     wq_sb[wbs[half]][:, kt * 128:(kt + 1) * 128],
                                     xtm_sb[b][:, kt * 512:(kt + 1) * 512],
                                     start=(kt == 0), stop=(kt == NKT - 1)),
                                 t3 if kt == NKT - 1 else None)
                        if kt == NKT - 1:
                            mm_done = n
                xtm_cons[b] = mm_done
                wq_cons[wbs[0]] = mm_done
                wq_cons[wbs[1]] = mm_done
                qb = hp % 2
                PA = [pAh, pBh][pp][0]
                PB = [pAh, pBh][pp][1]
                p.w("V", t3, mm_done)
                p.w("V", q3, tabd)
                p.w("V", g3, qo_st[qb])
                p.do("V", lambda e, hp=hp, PA=PA, b=b: e.scalar_tensor_tensor(
                    qtmp1[:], PA, bqtA[:, hp:hp + 1], tqa_sb[b][:], ALU.add, ALU.mult))
                p.do("V", lambda e, hp=hp, PB=PB, b=b: e.scalar_tensor_tensor(
                    qtmp2[:], PB, bqtB[:, hp:hp + 1], tqb_sb[b][:], ALU.add, ALU.mult))
                p.do("V", lambda e, qb=qb: e.tensor_sub(qlo_sb[qb][:], qtmp1[:], qtmp2[:]))
                p.do("V", lambda e, hp=hp, PA=PA, b=b: e.scalar_tensor_tensor(
                    qtmp1[:], PA, bqtA[:, hp:hp + 1], tqb_sb[b][:], ALU.add, ALU.mult))
                p.do("V", lambda e, hp=hp, PB=PB, b=b: e.scalar_tensor_tensor(
                    qtmp2[:], PB, bqtB[:, hp:hp + 1], tqa_sb[b][:], ALU.add, ALU.mult))
                bank_ev[pp] = p.do("V", lambda e, qb=qb: e.tensor_add(
                    qhi_sb[qb][:], qtmp1[:], qtmp2[:]), v3)
                evd = bank_ev[pp]
                p.w("S", v3, evd)
                p.do("S", lambda e, qb=qb: e.activation(qsqlo[qb][:], qlo_sb[qb][:], AF.Square))
                sqd = p.do("S", lambda e, qb=qb: e.activation(
                    qsqhi[qb][:], qhi_sb[qb][:], AF.Square), s3)
                nb = [pD, pE][hp % 2]
                p.w("T", s3, sqd)
                p.w("T", v3, norm_ev[hp % 2])
                p.do("T", lambda e, qb=qb, nb=nb: e.matmul(
                    nb[0:1, :], onescol[:], qsqlo[qb][:], start=True, stop=False,
                    skip_group_check=True))
                od = p.do("T", lambda e, qb=qb, nb=nb: e.matmul(
                    nb[0:1, :], onescol[:], qsqhi[qb][:], start=False, stop=True,
                    skip_group_check=True), t3)
                p.w("V", t3, od)
                norm_ev[hp % 2] = p.do("V", lambda e, nb=nb, sc=sc, hp=hp: e.reduce_max(
                    qmaxrow[:, sc * 16 + hp:sc * 16 + hp + 1], nb[0:1, :], axis=AX.X), v3)
                p.w("Q", v3, evd)
                for (buf, hoff) in ((qlo_sb, 0), (qhi_sb, 64)):
                    for two in range(2):
                        hh = 2 * hp + two
                        qo_st[qb] = 16 + g3.n
                        dma("Q", qTs[hh * 128 + hoff:hh * 128 + hoff + 64,
                                    sc * 512:(sc + 1) * 512],
                            buf[qb][two * 64:(two + 1) * 64, :], g3)
        p2_v_final = v3.n
        p2_g_final = g3.n

    if upto >= 5:
        q6 = p.sem("q6")
        v6 = p.sem("v6")
        g6 = p.sem("g6")
        d6 = p.sem("d6")
        s6b = p.sem("s6b")
        wf_sb = [_sb_at(nc, f"wfsb{i}", [128, 4096], F32, 128 * KB + i * 16 * KB)
                 for i in range(2)]
        qf_sb = _sb_at(nc, "qf_sb", [128, 4096], BF16, 160 * KB)
        btmp6 = _sb_at(nc, "btmp6", [128, 4096], BF16, 168 * KB)
        strip6 = [_sb_at(nc, f"strip6{i}", [128, 4096], BF16, 176 * KB + i * 8 * KB)
                  for i in range(2)]
        absb6 = _sb_at(nc, "absb6", [128, 4096], F32, 176 * KB)
        wf_cons = [0, 0]
        qf_cons = [0]
        strip_cons = [0, 0]

        def p6a(ot):
            wb = ot % 2

            def emit():
                p.w("Q", v6, wf_cons[wb])
                dma("Q", wf_sb[wb][:], Wf[ot * 128:(ot + 1) * 128, :], q6)
                ld = q6.n
                p.w("V", q6, ld)
                p.w("V", g6, qf_cons[0])
                p.do("V", lambda e, wb=wb: e.tensor_scalar_mul(
                    absb6[:], wf_sb[wb][:], -1.0))
                mx = p.do("V", lambda e, wb=wb: e.tensor_max(
                    absb6[:], absb6[:], wf_sb[wb][:]), v6)
                p.w("V", v6, mx)
                rd = p.do("V", lambda e: e.tensor_reduce(
                    stt1[:, 0:1], absb6[:], AX.X, ALU.add), v6)
                p.w("V", v6, rd)
                KSC = 4096.0 / 0.7
                sum_b = stt1[:, 0:1].broadcast_to([128, 4096])
                p.do("V", lambda e, wb=wb, sum_b=sum_b: e.scalar_tensor_tensor(
                    qf_sb[:], wf_sb[wb][:], KSC, sum_b, ALU.mult, ALU.is_gt))
                p.do("V", lambda e, wb=wb, sum_b=sum_b: e.scalar_tensor_tensor(
                    btmp6[:], wf_sb[wb][:], -KSC, sum_b, ALU.mult, ALU.is_gt))
                wf_cons[wb] = p.do("V", lambda e: e.tensor_sub(qf_sb[:], qf_sb[:], btmp6[:]), v6)
                p.w("G", v6, wf_cons[wb])
                qf_cons[0] = 16 + g6.n
                dma("G", qrow[ot * 128:(ot + 1) * 128, :], qf_sb[:], g6)
            return emit

        def p6b(kt):
            sb_ = kt % 2

            def emit():
                p.w("Q", g6, 32 * 16)
                p.w("Q", s6b, strip_cons[sb_])
                p.do("Q", lambda e, kt=kt, sb_=sb_: e.dma_start_transpose(
                    strip6[sb_][:], qrow[:, kt * 128:(kt + 1) * 128]), d6, 16)
                trd = d6.n
                p.w("G", d6, trd)
                strip_cons[sb_] = 16 + s6b.n
                dma("G", qWT[kt * 128:(kt + 1) * 128, :], strip6[sb_][:], s6b)
            return emit

    if upto >= 4:
        p.barrier()
        t4 = p.sem("t4")
        v4 = p.sem("v4")
        s4 = p.sem("s4")
        q4 = p.sem("q4")
        g4 = p.sem("g4")
        qc_sb = [_sb_at(nc, f"qc{i}", [128, H * 512], BF16, BASE + i * 32 * KB)
                 for i in range(2)]
        kg_sb = [_sb_at(nc, f"kg{i}", [128, S], BF16, 88 * KB + i * 8 * KB) for i in range(2)]
        vg_sb = [_sb_at(nc, f"vg{i}", [128, S], BF16, 104 * KB + i * 8 * KB) for i in range(2)]
        pt_sb = [_sb_at(nc, f"pt{i}", [128, 1024], BF16, 120 * KB + i * 2 * KB) for i in range(2)]
        recb = _sb_at(nc, "recb", [128, 512], F32, 124 * KB)
        ctx_sb = [_sb_at(nc, f"ctx{i}", [128, 512], BF16, 126 * KB + i * KB) for i in range(2)]

        p.w("V", v3, p2_v_final)
        p.w("V", v1, p1a_km_final)
        vnu = p.do("V", lambda e: e.tensor_scalar(
            rtmp1, qmaxrow, km2, None, ALU.mult), v4)
        p.w("V", v4, vnu)
        first = True
        for j in (0, 2, 4, 6, 8, 10):
            c = float(2 ** j)
            n_ = p.do("V", lambda e, c=c: e.tensor_scalar(
                rtmp2, rtmp1, 0.5 / c, 0.5 * c, ALU.mult, ALU.add), v4)
            p.w("V", v4, n_)
            if first:
                n_ = p.do("V", lambda e: e.tensor_copy(negu_row, rtmp2), v4)
                first = False
            else:
                n_ = p.do("V", lambda e: e.tensor_tensor(
                    negu_row, negu_row, rtmp2, ALU.min), v4)
            p.w("V", v4, n_)
        nr = p.do("V", lambda e: e.tensor_scalar(
            negu_row, negu_row, -1.01, -1.0, ALU.mult, ALU.add), v4)
        p.w("V", v4, nr)
        vnu2 = p.do("V", lambda e: e.tensor_copy(negu_bf[:], negu_row), v4)
        p.w("T", v4, vnu2)
        tnu = p.do("T", lambda e: e.matmul(
            pE[:, 0:64], onesrow_bf[:], negu_bf[:], start=True, stop=True), t4)
        p.w("S", t4, tnu)
        negub_done = p.do("S", lambda e: e.copy(negUb[:], pE[:, 0:64]), s4)

        p6a_list = [p6a(ot) for ot in range(32)] if upto >= 5 else []
        p6b_list = [p6b(kt) for kt in range(NKT)] if upto >= 5 else []
        p6a_i = 0
        p6b_i = 0

        qc_cons = [0, 0]
        kg_cons = [0, 0]
        vg_cons = [0, 0]
        pt_cons = [0, 0]
        pt_mask_req = [None, None]
        sc_ev = [0, 0]
        ctx_ev = [0, 0]
        sum_ev = 0
        rec_ev = 0
        ctx_st = [0, 0]
        qrow_targets = []

        for c in range(4):
            Tc = (8 * c + 8) * 128
            npair = 4 * c + 4
            cpar = c % 2
            p.w("Q", t4, qc_cons[cpar])
            p.w("Q", g3 if upto >= 3 else gI, p2_g_final if upto >= 3 else 0)
            dma("Q", qc_sb[cpar][:].rearrange("p (h s) -> p h s", s=512),
                qTs.rearrange("(h p) s -> p h s", p=128)[:, :, c * 512:(c + 1) * 512], q4)
            qc_ld = q4.n
            for g in range(G):
                gb = g % 2
                p.w("Q", t4, kg_cons[gb])
                p.w("Q", g1 if upto >= 1 else gI, p1a_g_final if upto >= 1 else 0)
                dma("Q", kg_sb[gb][:, 0:Tc], kT[g * 128:(g + 1) * 128, 0:Tc], q4)
                p.w("Q", t4, vg_cons[gb])
                p.w("Q", g2 if upto >= 2 else gI, p1b_g_final if upto >= 2 else 0)
                dma("Q", vg_sb[gb][:].rearrange("p (tt d) -> p tt d", d=128)[:, 0:Tc // 128, :],
                    v2[0:Tc, g * 128:(g + 1) * 128].rearrange("(tt t) d -> t tt d", t=128),
                    q4)
                kv_ld = q4.n
                for r in range(4):
                    h = g * 4 + r
                    cr = (g * 4 + r) % 2

                    def emit_scores(pr, gb=gb, h=h, cpar=cpar):
                        s0 = 128 * max(0, pr - 4 * c)
                        pb = pr % 2
                        p.w("T", q4, max(qc_ld, kv_ld))
                        p.w("T", s4, sc_ev[pb])
                        p.w("T", s4, negub_done)
                        p.do("T", lambda e, pr=pr, s0=s0, gb=gb, h=h, pb=pb, cpar=cpar:
                             e.matmul(
                                 [pA, pB][pb][:, s0:512],
                                 kg_sb[gb][:, (2 * pr) * 128:(2 * pr + 1) * 128],
                                 qc_sb[cpar][:, h * 512 + s0:h * 512 + 512],
                                 start=True, stop=True))
                        scd = p.do("T", lambda e, pr=pr, s0=s0, gb=gb, h=h, pb=pb,
                                   cpar=cpar: e.matmul(
                                       [pA, pB][pb][:, 512 + s0:1024],
                                       kg_sb[gb][:, (2 * pr + 1) * 128:(2 * pr + 2) * 128],
                                       qc_sb[cpar][:, h * 512 + s0:h * 512 + 512],
                                       start=True, stop=True), t4)
                        kg_cons[gb] = scd
                        qc_cons[cpar] = scd
                        return scd

                    scd_next = emit_scores(0)
                    for pr in range(npair):
                        l0 = max(0, pr - 4 * c)
                        s0 = 128 * l0
                        pb = pr % 2
                        scd = scd_next
                        p.w("S", t4, scd)
                        if pt_cons[pb]:
                            p.w("S", t4, pt_cons[pb])
                        if pt_mask_req[pb] is not None:
                            p.w("S", v4, pt_mask_req[pb])
                        expd = p.do("S", lambda e, pb=pb, s0=s0, c=c, h=h: e.activation(
                            pt_sb[pb][:].rearrange("p (two x) -> p two x", two=2)[:, :, s0:512],
                            [pA, pB][pb][:].rearrange("p (two x) -> p two x", two=2)[:, :, s0:512],
                            AF.Exp, bias=negUb[:, c * 16 + h // 2:c * 16 + h // 2 + 1],
                            scale=1.0), s4)
                        sc_ev[pb] = expd
                        masked = (4 * c <= pr <= 4 * c + 3)
                        if masked:
                            l = pr - 4 * c
                            p.w("V", s4, expd)
                            p.do("V", lambda e, pb=pb, l=l, pr=pr: e.tensor_mul(
                                pt_sb[pb][:, l * 128:(l + 1) * 128],
                                pt_sb[pb][:, l * 128:(l + 1) * 128],
                                dmask[:, (pr * 2) * 128:(pr * 2 + 1) * 128]))
                            mkd = p.do("V", lambda e, pb=pb, l=l, pr=pr: e.tensor_mul(
                                pt_sb[pb][:, 512 + l * 128:512 + (l + 1) * 128],
                                pt_sb[pb][:, 512 + l * 128:512 + (l + 1) * 128],
                                dmask[:, (pr * 2 + 1) * 128:(pr * 2 + 2) * 128]), v4)
                            pt_mask_req[pb] = mkd
                        else:
                            pt_mask_req[pb] = None
                        if pr + 1 < npair:
                            scd_next = emit_scores(pr + 1)
                        p.w("T", s4, expd)
                        if masked:
                            p.w("T", v4, mkd)
                        if pr == 0:
                            p.w("T", v4, ctx_ev[cr])
                            p.w("T", v4, sum_ev)
                        p.do("T", lambda e, pr=pr, s0=s0, gb=gb, pb=pb, cr=cr: e.matmul(
                            pCh[cr][:, s0:512],
                            vg_sb[gb][:, (2 * pr) * 128:(2 * pr + 1) * 128],
                            pt_sb[pb][:, s0:512], start=(pr == 0), stop=False,
                            skip_group_check=True))
                        pvd = p.do("T", lambda e, pr=pr, s0=s0, gb=gb, pb=pb, cr=cr:
                                   e.matmul(
                                       pCh[cr][:, s0:512],
                                       vg_sb[gb][:, (2 * pr + 1) * 128:(2 * pr + 2) * 128],
                                       pt_sb[pb][:, 512 + s0:1024], start=False,
                                       stop=(pr == npair - 1), skip_group_check=True), t4)
                        vg_cons[gb] = pvd
                        p.do("T", lambda e, pr=pr, s0=s0, pb=pb: e.matmul(
                            pD[0:1, s0:512], onescol[:], pt_sb[pb][:, s0:512],
                            start=(pr == 0), stop=False, skip_group_check=True))
                        oned = p.do("T", lambda e, pr=pr, s0=s0, pb=pb: e.matmul(
                            pD[0:1, s0:512], onescol[:], pt_sb[pb][:, 512 + s0:1024],
                            start=False, stop=(pr == npair - 1),
                            skip_group_check=True), t4)
                        pt_cons[pb] = oned
                    p.w("V", t4, pt_cons[0])
                    p.w("V", t4, pt_cons[1])
                    sum_ev = p.do("V", lambda e: e.reciprocal(rec_row, pD[0:1, :]), v4)
                    p.w("T", v4, max(sum_ev, rec_ev))
                    bcd = p.do("T", lambda e: e.matmul(
                        pE[:], onesrow_f[:], rec_row, start=True, stop=True), t4)
                    cb = (g * 4 + r) % 2
                    p.w("V", t4, bcd)
                    p.w("V", g4, ctx_st[cb])
                    rec_ev = p.do("V", lambda e: e.tensor_copy(recb[:], pE[:]), v4)
                    ctx_ev[cr] = p.do("V", lambda e, cb=cb, cr=cr: e.tensor_mul(
                        ctx_sb[cb][:], pCh[cr][:], recb[:]), v4)
                    p.w("G", v4, ctx_ev[cr])
                    ctx_st[cb] = 16 + g4.n
                    dma("G", ctxT[h * 128:(h + 1) * 128, c * 512:(c + 1) * 512],
                        ctx_sb[cb][:], g4)
                it = c * 8 + g
                if upto >= 5:
                    if c < 3:
                        want = min(32, round((it + 1) * 32 / 24))
                        while p6a_i < want:
                            qrow_targets.append(p6a_list[p6a_i]())
                            p6a_i += 1
                    else:
                        while p6a_i < 32:
                            qrow_targets.append(p6a_list[p6a_i]())
                            p6a_i += 1
                        want_b = min(NKT, (g + 1) * 4)
                        while p6b_i < want_b:
                            p6b_list[p6b_i]()
                            p6b_i += 1
        p3_g_final = g4.n
        if upto >= 5:
            p6_d_final = s6b.n

    if upto >= 6:
        p.barrier()
        t5 = p.sem("t5")
        v5 = p.sem("v5")
        s5 = p.sem("s5")
        q5 = p.sem("q5")
        g5 = p.sem("g5")
        ctxh_sb = _sb_at(nc, "ctxh", [128, H * 1024], BF16, BASE)
        wo_sb = [_sb_at(nc, f"wosb{i}", [128, NKT * 512], BF16, 88 * KB + i * 32 * KB)
                 for i in range(2)]
        bo_sl = [_sb_at(nc, f"bosl{i}", [128, 512], F32, 152 * KB + i * 2 * KB) for i in range(2)]
        xm_sl = [_sb_at(nc, f"xmsl{i}", [128, 512], F32, 156 * KB + i * 2 * KB) for i in range(2)]
        tmpf4 = _sb_at(nc, "tmpf4", [128, 512], F32, 160 * KB)
        x1c_sb = [_sb_at(nc, f"x1c{i}", [128, 512], F32, 162 * KB + i * 2 * KB) for i in range(2)]
        junkb = _sb_at(nc, "junkb4", [128, 4096], BF16, 166 * KB)
        xp_full = _sb_at(nc, "xp4", [128, 4096], F32, 174 * KB)
        xr_full = _sb_at(nc, "xr4", [128, 4096], BF16, 190 * KB)
        woc_done = woc_done_g

        def ln_stripe5(st8, grp, gdg):
            st = grp * 8 + st8
            nonlocal xp_cons, xr_st
            p.w("Q", v5, xp_cons)
            p.w("Q", g5, gdg)
            dma("Q", xp_full[:], x1pre[st * 128:(st + 1) * 128, :], q5)
            xld = q5.n
            p.w("V", q5, xld)
            n_ = p.do("V", lambda e: e.tensor_reduce(
                stt1[:, 0:1], xp_full[:], AX.X, ALU.add), v5)
            p.w("V", v5, n_)
            n_ = p.do("V", lambda e: e.scalar_tensor_tensor(
                junkb[:], xp_full[:], 1.0, xp_full[:], ALU.mult, ALU.mult,
                accum_out=stt1[:, 1:2]), v5)
            p.w("V", v5, n_)
            n_ = p.do("V", lambda e: e.tensor_scalar_mul(
                stt1[:, 2:3], stt1[:, 0:1], 1.0 / 4096.0), v5)
            p.w("V", v5, n_)
            n_ = p.do("V", lambda e: e.tensor_scalar_mul(
                stt1[:, 3:4], stt1[:, 1:2], 1.0 / 4096.0), v5)
            p.w("V", v5, n_)
            n_ = p.do("V", lambda e: e.tensor_mul(
                stt2[:, 0:1], stt1[:, 2:3], stt1[:, 2:3]), v5)
            p.w("V", v5, n_)
            n_ = p.do("V", lambda e: e.tensor_sub(
                stt2[:, 1:2], stt1[:, 3:4], stt2[:, 0:1]), v5)
            p.w("V", v5, n_)
            n_ = p.do("V", lambda e: e.tensor_scalar_add(
                stt2[:, 1:2], stt2[:, 1:2], EPS), v5)
            p.w("V", v5, n_)
            n_ = p.do("V", lambda e: e.reciprocal(stt2[:, 2:3], stt2[:, 1:2]), v5)
            p.w("V", v5, n_)
            n_ = p.do("V", lambda e: e.tensor_scalar_min(
                stt2[:, 3:4], stt2[:, 2:3], 1.0), v5)
            p.w("V", v5, n_)
            for _it in range(12):
                n_ = p.do("V", lambda e: e.tensor_mul(
                    stt3[:, 0:1], stt2[:, 3:4], stt2[:, 3:4]), v5)
                p.w("V", v5, n_)
                n_ = p.do("V", lambda e: e.tensor_mul(
                    stt3[:, 0:1], stt3[:, 0:1], stt2[:, 1:2]), v5)
                p.w("V", v5, n_)
                n_ = p.do("V", lambda e: e.tensor_scalar(
                    stt3[:, 0:1], stt3[:, 0:1], -0.5, 1.5, ALU.mult, ALU.add), v5)
                p.w("V", v5, n_)
                n_ = p.do("V", lambda e: e.tensor_mul(
                    stt2[:, 3:4], stt2[:, 3:4], stt3[:, 0:1]), v5)
                p.w("V", v5, n_)
            p.w("V", g5, xr_st)
            xp_cons = p.do("V", lambda e: e.tensor_scalar(
                xr_full[:], xp_full[:], stt1[:, 2:3], stt2[:, 3:4],
                ALU.subtract, ALU.mult), v5)
            p.w("G", v5, xp_cons)
            xr_st = 16 + g5.n
            dma("G",
                x1row[st * 128:(st + 1) * 128, :], xr_full[:], g5)
        pending_ln = []
        ctxh_cons = 0
        wo_cons = [0, 0]
        bank_ev4 = [0, 0, 0, 0]
        bo_cons = [0, 0]
        xm_cons = [0, 0]
        x1c_st = [0, 0]
        xp_cons = 0
        xr_st = 0
        for grp in range(2):
            p.w("Q", t5, ctxh_cons)
            p.w("Q", g4, p3_g_final)
            dma("Q", ctxh_sb[:].rearrange("p (h s) -> p h s", s=1024),
                ctxT.rearrange("(h p) s -> p h s", p=128)[:, :, grp * 1024:(grp + 1) * 1024],
                q5)
            ctxh_ld = q5.n
            for nch in range(8):
                wb = nch % 2
                p.w("Q", gI, woc_done)
                p.w("Q", t5, wo_cons[wb])
                dma("Q", wo_sb[wb][:], wo_bf[nch * 128:(nch + 1) * 128, :], q5)
                wold = q5.n
                p.w("G", v5, bo_cons[wb])
                dma("G", bo_sl[wb][:], bass.AP(
                    bo_i.tensor, nch * 512, [[0, 128], [1, 512]]), g5)
                bold = g5.n
                for st8 in range(8):
                    st = grp * 8 + st8
                    i4 = (nch * 8 + st8) % 4
                    ldb = (nch * 8 + st8) % 2
                    p.w("Q", q5, 0)
                    p.w("Q", v5, xm_cons[ldb])
                    dma("Q", xm_sl[ldb][:],
                        x_mine[st * 128:(st + 1) * 128, nch * 512:(nch + 1) * 512], q5)
                    xmld = q5.n
                    p.w("T", q5, max(ctxh_ld, wold))
                    p.w("T", v5, bank_ev4[i4])
                    for hd in range(32):
                        n = p.do("T", lambda e, hd=hd, st8=st8, wb=wb, i4=i4: e.matmul(
                            p4rot[i4],
                            ctxh_sb[:, hd * 1024 + st8 * 128:hd * 1024 + (st8 + 1) * 128],
                            wo_sb[wb][:, hd * 512:(hd + 1) * 512],
                            start=(hd == 0), stop=(hd == 31)),
                            t5 if hd == 31 else None)
                        if hd == 31:
                            mmd = n
                    ctxh_cons = mmd
                    wo_cons[wb] = mmd
                    xc = (nch * 8 + st8) % 2
                    p.w("V", t5, mmd)
                    p.w("V", g5, bold)
                    p.w("V", q5, xmld)
                    p.w("V", g5, x1c_st[xc])
                    p.do("V", lambda e, i4=i4, wb=wb: e.tensor_add(
                        tmpf4[:], p4rot[i4], bo_sl[wb][:]))
                    ev = p.do("V", lambda e, ldb=ldb, xc=xc:
                              e.scalar_tensor_tensor(
                                  x1c_sb[xc][:], tmpf4[:], 0.0, xm_sl[ldb][:],
                                  ALU.add, ALU.add),
                              v5)
                    bank_ev4[i4] = ev
                    xm_cons[ldb] = ev
                    bo_cons[wb] = ev
                    p.w("G", v5, ev)
                    x1c_st[xc] = 16 + g5.n
                    dma("G", x1pre[st * 128:(st + 1) * 128, nch * 512:(nch + 1) * 512],
                        x1c_sb[xc][:], g5)
                    if pending_ln and st8 == 7:
                        ln_stripe5(*pending_ln.pop(0))
            grp_done_v = v5.n
            grp_done_s = s5.n
            grp_done_g = g5.n
            if grp == 0:
                pending_ln = [(st8, grp, grp_done_g) for st8 in range(8)]
            else:
                for args in pending_ln:
                    ln_stripe5(*args)
                pending_ln = []
                for st8 in range(8):
                    ln_stripe5(st8, 1, grp_done_g)
        p45_g_final = g5.n

    if upto >= 7:
        p.barrier()
        t7 = p.sem("t7")
        v7 = p.sem("v7")
        s7 = p.sem("s7")
        q7 = p.sem("q7")
        g7 = p.sem("g7")
        q8 = p.sem("q8")
        x1T_sb = _sb_at(nc, "x1T", [128, NKT * 1024], BF16, BASE)
        qw_sb = [_sb_at(nc, f"qwsb{i}", [128, NKT * 512], BF16, 88 * KB + i * 32 * KB)
                 for i in range(2)]
        fb_sl = [_sb_at(nc, f"fbsl{i}", [128, 512], F32, 152 * KB + i * 2 * KB) for i in range(2)]
        x1s_sl = [_sb_at(nc, f"x1ssl{i}", [128, 512], BF16, 156 * KB + i * KB) for i in range(2)]
        tmpf7 = _sb_at(nc, "tmpf7", [128, 512], F32, 158 * KB)
        x2c_sb = [_sb_at(nc, f"x2c{i}", [128, 512], F32, 160 * KB + i * 2 * KB) for i in range(2)]
        junkb7 = _sb_at(nc, "junkb7", [128, 4096], BF16, 164 * KB)
        xp_full7 = _sb_at(nc, "xp7", [128, 4096], F32, 172 * KB)
        xr_full7 = _sb_at(nc, "xr7", [128, 4096], F32, 188 * KB)

        def ln_stripe7(st8, grp, gdg):
            st = grp * 8 + st8
            nonlocal xp_cons, xr_st
            p.w("Q", v7, xp_cons)
            p.w("Q", g7, gdg)
            dma("Q", xp_full7[:], x2pre[st * 128:(st + 1) * 128, :], q7)
            xld = q7.n
            p.w("V", q7, xld)
            n_ = p.do("V", lambda e: e.tensor_reduce(
                stt1[:, 0:1], xp_full7[:], AX.X, ALU.add), v7)
            p.w("V", v7, n_)
            n_ = p.do("V", lambda e: e.scalar_tensor_tensor(
                junkb7[:], xp_full7[:], 1.0, xp_full7[:], ALU.mult, ALU.mult,
                accum_out=stt1[:, 1:2]), v7)
            p.w("V", v7, n_)
            n_ = p.do("V", lambda e: e.tensor_scalar_mul(
                stt1[:, 2:3], stt1[:, 0:1], 1.0 / 4096.0), v7)
            p.w("V", v7, n_)
            n_ = p.do("V", lambda e: e.tensor_scalar_mul(
                stt1[:, 3:4], stt1[:, 1:2], 1.0 / 4096.0), v7)
            p.w("V", v7, n_)
            n_ = p.do("V", lambda e: e.tensor_mul(
                stt2[:, 0:1], stt1[:, 2:3], stt1[:, 2:3]), v7)
            p.w("V", v7, n_)
            n_ = p.do("V", lambda e: e.tensor_sub(
                stt2[:, 1:2], stt1[:, 3:4], stt2[:, 0:1]), v7)
            p.w("V", v7, n_)
            n_ = p.do("V", lambda e: e.tensor_scalar_add(
                stt2[:, 1:2], stt2[:, 1:2], EPS), v7)
            p.w("V", v7, n_)
            n_ = p.do("V", lambda e: e.reciprocal(stt2[:, 2:3], stt2[:, 1:2]), v7)
            p.w("V", v7, n_)
            n_ = p.do("V", lambda e: e.tensor_scalar_min(
                stt2[:, 3:4], stt2[:, 2:3], 1.0), v7)
            p.w("V", v7, n_)
            for _it in range(18):
                n_ = p.do("V", lambda e: e.tensor_mul(
                    stt3[:, 0:1], stt2[:, 3:4], stt2[:, 3:4]), v7)
                p.w("V", v7, n_)
                n_ = p.do("V", lambda e: e.tensor_mul(
                    stt3[:, 0:1], stt3[:, 0:1], stt2[:, 1:2]), v7)
                p.w("V", v7, n_)
                n_ = p.do("V", lambda e: e.tensor_scalar(
                    stt3[:, 0:1], stt3[:, 0:1], -0.5, 1.5, ALU.mult, ALU.add), v7)
                p.w("V", v7, n_)
                n_ = p.do("V", lambda e: e.tensor_mul(
                    stt2[:, 3:4], stt2[:, 3:4], stt3[:, 0:1]), v7)
                p.w("V", v7, n_)
            p.w("V", q8, xr_st)
            xp_cons = p.do("V", lambda e: e.tensor_scalar(
                xr_full7[:], xp_full7[:], stt1[:, 2:3], stt2[:, 3:4],
                ALU.subtract, ALU.mult), v7)
            p.w("Q", v7, xp_cons)
            xr_st = 16 + q8.n
            dma("Q", out_c[st * 128:(st + 1) * 128, :], xr_full7[:], q8)
        pending_ln7 = []
        x1t_cons = 0
        qw_cons = [0, 0]
        bank_ev7 = [0, 0, 0, 0]
        fb_cons = [0, 0]
        x1s_cons = [0, 0]
        x2c_st = [0, 0]
        xp_cons = 0
        xr_st = 0
        for grp in range(2):
            for kt in range(NKT):
                eng = "Q"
                p.w(eng, g5, p45_g_final)
                if grp == 1:
                    p.w(eng, t7, x1t_cons)
                p.do(eng, lambda e, kt=kt, grp=grp: e.dma_start_transpose(
                    x1T_sb[:, kt * 1024:(kt + 1) * 1024],
                    x1row[grp * 1024:(grp + 1) * 1024, kt * 128:(kt + 1) * 128]), q7, 16)
            x1trd = q7.n
            for och in range(8):
                wb = och % 2
                p.w("Q", s6b, p6_d_final)
                p.w("Q", t7, qw_cons[wb])
                dma("Q", qw_sb[wb][:].rearrange("p (kt n) -> p kt n", n=512),
                    qWT.rearrange("(kt p) o -> p kt o", p=128)[:, :, och * 512:(och + 1) * 512],
                    q7)
                qwld = q7.n
                p.w("G", v7, fb_cons[wb])
                dma("G", fb_sl[wb][:], bass.AP(
                    fb_i.tensor, och * 512, [[0, 128], [1, 512]]), g7)
                fbld = g7.n
                for st8 in range(8):
                    st = grp * 8 + st8
                    i7 = (och * 8 + st8) % 4
                    ldb = (och * 8 + st8) % 2
                    p.w("Q", v7, x1s_cons[ldb])
                    dma("Q", x1s_sl[ldb][:],
                        x1row[st * 128:(st + 1) * 128, och * 512:(och + 1) * 512], q7)
                    x1sld = q7.n
                    p.w("T", q7, max(x1trd, qwld))
                    p.w("T", v7, bank_ev7[i7])
                    for kt in range(NKT):
                        n = p.do("T", lambda e, kt=kt, st8=st8, wb=wb, i7=i7: e.matmul(
                            p4rot[i7],
                            x1T_sb[:, kt * 1024 + st8 * 128:kt * 1024 + (st8 + 1) * 128],
                            qw_sb[wb][:, kt * 512:(kt + 1) * 512],
                            start=(kt == 0), stop=(kt == NKT - 1)),
                            t7 if kt == NKT - 1 else None)
                        if kt == NKT - 1:
                            mmd = n
                    x1t_cons = mmd
                    qw_cons[wb] = mmd
                    xc = (och * 8 + st8) % 2
                    p.w("V", t7, mmd)
                    p.w("V", g7, fbld)
                    p.w("V", q7, x1sld)
                    p.w("V", g7, x2c_st[xc])
                    p.do("V", lambda e, i7=i7, wb=wb: e.tensor_add(
                        tmpf7[:], p4rot[i7], fb_sl[wb][:]))
                    ev = p.do("V", lambda e, ldb=ldb, xc=xc:
                              e.scalar_tensor_tensor(
                                  x2c_sb[xc][:], tmpf7[:], 0.0, x1s_sl[ldb][:],
                                  ALU.add, ALU.add),
                              v7)
                    bank_ev7[i7] = ev
                    x1s_cons[ldb] = ev
                    fb_cons[wb] = ev
                    p.w("G", v7, ev)
                    x2c_st[xc] = 16 + g7.n
                    dma("G", x2pre[st * 128:(st + 1) * 128, och * 512:(och + 1) * 512],
                        x2c_sb[xc][:], g7)
                    if pending_ln7 and st8 == 7:
                        ln_stripe7(*pending_ln7.pop(0))
            grp_done_v = v7.n
            grp_done_s = s7.n
            grp_done_g = g7.n
            if grp == 0:
                pending_ln7 = [(st8, grp, grp_done_g) for st8 in range(8)]
            else:
                for args in pending_ln7:
                    ln_stripe7(*args)
                pending_ln7 = []
                for st8 in range(8):
                    ln_stripe7(st8, 1, grp_done_g)

    for s in p.sems:
        if s.n > 0:
            p.w("Q", s, s.n)

    with nc.Block() as block:
        @block.tensor
        def _(e):
            for f in p.st["T"]:
                f(e)

        @block.vector
        def _(e):
            for f in p.st["V"]:
                f(e)

        @block.scalar
        def _(e):
            for f in p.st["S"]:
                f(e)

        @block.sync
        def _(e):
            for f in p.st["Q"]:
                f(e)

        @block.gpsimd
        def _(e):
            for f in p.st["G"]:
                f(e)

    es.close()
    return nc


_BF = ml_dtypes.bfloat16


def _host_tables():
    inv = (THETA ** (-np.arange(64, dtype=np.float64) / 64.0))
    pos = np.arange(S, dtype=np.float64)
    ang = inv[:, None] * pos[None, :]
    cosk = np.cos(ang).astype(np.float32)
    sink = np.sin(ang).astype(np.float32)
    tabKA = np.concatenate([cosk, cosk], axis=0)
    tabKB = np.concatenate([sink, sink], axis=0)
    return tabKA, tabKB, cosk, sink


def _host_qtables(stripes, cosk, sink):
    cols = np.concatenate([np.arange(p * ST, (p + 1) * ST) for p in stripes])
    sc = np.float32(1.0 / np.sqrt(D))
    cq = cosk[:, cols] * sc
    sq = sink[:, cols] * sc
    return (np.concatenate([cq, cq], axis=0).astype(np.float32),
            np.concatenate([sq, sq], axis=0).astype(np.float32))


def _perm_cols(W, npair):
    r = W.reshape(W.shape[0], npair, 2, 2, 64)
    return np.ascontiguousarray(r.transpose(0, 1, 3, 2, 4).reshape(W.shape))


def _perm_bias(bvec, npair):
    b = np.asarray(bvec, np.float32).reshape(npair, 2, 2, 64)
    return (np.ascontiguousarray(b[:, :, 0, :].reshape(npair, 128)),
            np.ascontiguousarray(b[:, :, 1, :].reshape(npair, 128)))


def _host_masks(stripes):
    tri = np.triu(np.ones((128, 128), np.float32))
    m = np.zeros((MYST, 2, 128, 128), np.float32)
    for j, pos in enumerate(stripes):
        if pos == 2 * j + 1:
            m[j, 0] = 1.0
            m[j, 1] = tri
        else:
            m[j, 0] = tri
            m[j, 1] = 0.0
    return np.ascontiguousarray(m.transpose(2, 0, 1, 3).reshape(128, MYST * 2 * 128)
                                ).astype(_BF)


_CACHE = {}


def _get_nc():
    if "nc" not in _CACHE:
        _CACHE["nc"] = build_program()
    return _CACHE["nc"]


def make_in_maps(L0_attn_hidden_states, L0_attn_Wq, L0_attn_bq, L0_attn_Wk,
                 L0_attn_bk, L0_attn_Wv, L0_attn_bv, L0_attn_Wo, L0_attn_bo,
                 L0_ffn_weight, L0_ffn_bias):
    x = np.ascontiguousarray(np.asarray(L0_attn_hidden_states, dtype=np.float32))
    tabKA, tabKB, cosk, sink = _host_tables()
    bqA, bqB = _perm_bias(L0_attn_bq, 16)
    bkA, bkB = _perm_bias(L0_attn_bk, 4)
    shared = {
        "Wq": _perm_cols(np.asarray(L0_attn_Wq, np.float32), 16),
        "Wk": _perm_cols(np.asarray(L0_attn_Wk, np.float32), 4),
        "Wv": np.ascontiguousarray(np.asarray(L0_attn_Wv, np.float32)),
        "Wo": np.ascontiguousarray(np.asarray(L0_attn_Wo, np.float32)),
        "Wf": np.ascontiguousarray(np.asarray(L0_ffn_weight, np.float32)),
        "bqA_r": bqA, "bqB_r": bqB, "bkA_r": bkA, "bkB_r": bkB,
        "bv_i": np.asarray(L0_attn_bv, np.float32).reshape(1, -1),
        "bo_i": np.asarray(L0_attn_bo, np.float32).reshape(1, -1),
        "fb_i": np.asarray(L0_ffn_bias, np.float32).reshape(1, -1),
        "tabKA": tabKA, "tabKB": tabKB,
        "identf": np.eye(128, dtype=np.float32),
        "onescol_i": np.ones((128, 1), _BF),
        "onesrow_bf_i": np.ones((1, 128), _BF),
        "onesrow_f_i": np.ones((1, 128), np.float32),
    }
    sets = [SET_A, SET_B]
    qtabs = [_host_qtables(s, cosk, sink) for s in sets]
    masks = [_host_masks(s) for s in sets]
    in_maps = []
    for core in range(8):
        b = core // 2
        h = core % 2
        stripes = sets[h]
        xb = x[b]
        xm = np.ascontiguousarray(
            xb.reshape(NST, ST, HID)[stripes].reshape(MYST * ST, HID))
        m = dict(shared)
        m["x_full"] = xb
        m["x_mine"] = xm
        m["tabQA"], m["tabQB"] = qtabs[h]
        m["dmask_i"] = masks[h]
        in_maps.append(m)
    return in_maps


def kernel(**inputs):
    in_maps = make_in_maps(**inputs)
    nc = _get_nc()
    res = run_bass_kernel_spmd(nc, in_maps, core_ids=list(range(8)))
    out = np.empty((B, S, HID), dtype=np.float32)
    sets = [SET_A, SET_B]
    for core in range(8):
        b = core // 2
        h = core % 2
        stripes = sets[h]
        oc = np.asarray(res.results[core]["out_c"]).reshape(MYST, ST, HID)
        for j, pos in enumerate(stripes):
            out[b, pos * ST:(pos + 1) * ST, :] = oc[j]
    return out


# version 06
# speedup vs baseline: 1.1091x; 1.0031x over previous
import contextlib
import numpy as np
import ml_dtypes
import concourse.bass as bass
import concourse.mybir as mybir
from concourse.bass_utils import run_bass_kernel_spmd

F32 = mybir.dt.float32
BF16 = mybir.dt.bfloat16
AF = mybir.ActivationFunctionType
ALU = mybir.AluOpType
AX = mybir.AxisListType

B, S, HID = 4, 4096, 4096
H, G, D = 32, 8, 128
NST = 32
MYST = 16
ST = 128
NKT = 32
THETA = 10000.0
EPS = 1e-5

SET_A = sorted([4 * k for k in range(8)] + [4 * k + 3 for k in range(8)])
SET_B = sorted([4 * k + 1 for k in range(8)] + [4 * k + 2 for k in range(8)])
for _j, _p in enumerate(SET_A):
    assert _p in (2 * _j, 2 * _j + 1)
for _j, _p in enumerate(SET_B):
    assert _p in (2 * _j, 2 * _j + 1)

ENGS = "TVSQG"


class Sem:
    __slots__ = ("h", "n")

    def __init__(self, h):
        self.h = h
        self.n = 0


class Prog:

    def __init__(self, nc, es):
        self.nc = nc
        self.es = es
        self.st = {k: [] for k in ENGS}
        self.sems = []

    def sem(self, name):
        s = Sem(self.es.enter_context(self.nc.semaphore(name)))
        self.sems.append(s)
        return s

    def barrier(self):
        for eng in ENGS:
            for s in self.sems:
                if s.n > 0:
                    self.w(eng, s, s.n)

    def w(self, eng, sem, val):
        if val > 0:
            self.st[eng].append(lambda e, s=sem, v=val: e.wait_ge(s.h, v))

    def do(self, eng, fn, inc=None, amt=1):
        if inc is not None:
            inc.n += amt
        self.st[eng].append(
            lambda e, f=fn, i=inc, a=amt:
            (f(e).then_inc(i.h, a) if i is not None else f(e)))
        return inc.n if inc is not None else None


SB0 = 16384


def _sb_at(nc, name, shape, dtype, off_bytes):
    return nc.alloc_sbuf_tensor_at(name, shape, dtype, offset=SB0 + off_bytes)


def build_program(dbg=False, upto=99):
    nc = bass.Bass("TRN2", target_bir_lowering=False, debug=False)
    es = contextlib.ExitStack()
    p = Prog(nc, es)
    din = lambda n, sh, dt: nc.dram_tensor(n, sh, dt, kind="ExternalInput").ap()
    dout = lambda n, sh, dt: nc.dram_tensor(n, sh, dt, kind="ExternalOutput").ap()
    dscr = lambda n, sh, dt: nc.dram_tensor(
        n, sh, dt, kind="ExternalOutput" if dbg else "Internal").ap()

    x_full = din("x_full", [S, HID], F32)
    x_mine = din("x_mine", [MYST * ST, HID], F32)
    Wq = din("Wq", [HID, H * D], F32)
    Wk = din("Wk", [HID, G * D], F32)
    Wv = din("Wv", [HID, G * D], F32)
    Wo = din("Wo", [H * D, HID], F32)
    Wf = din("Wf", [HID, HID], F32)
    bqA_r = din("bqA_r", [16, 128], F32)
    bqB_r = din("bqB_r", [16, 128], F32)
    bkA_r = din("bkA_r", [4, 128], F32)
    bkB_r = din("bkB_r", [4, 128], F32)
    bv_i = din("bv_i", [1, G * D], F32)
    bo_i = din("bo_i", [1, HID], F32)
    fb_i = din("fb_i", [1, HID], F32)
    tabKA = din("tabKA", [128, S], F32)
    tabKB = din("tabKB", [128, S], F32)
    tabQA = din("tabQA", [128, MYST * ST], F32)
    tabQB = din("tabQB", [128, MYST * ST], F32)
    dmask_i = din("dmask_i", [128, MYST * 2 * 128], BF16)
    identf = din("identf", [128, 128], F32)
    onescol_i = din("onescol_i", [128, 1], BF16)
    onesrow_bf_i = din("onesrow_bf_i", [1, 128], BF16)
    onesrow_f_i = din("onesrow_f_i", [1, 128], F32)
    out_c = dout("out_c", [MYST * ST, HID], F32)

    xbf = dscr("xbf", [S, HID], BF16)
    xbfm = dscr("xbfm", [MYST * ST, HID], BF16)
    wq_bf = dscr("wq_bf", [H * D, HID], BF16)
    wo_bf = dscr("wo_bf", [8 * 128, NKT * 512], BF16)
    kT = dscr("kT", [G * 128, S], BF16)
    v2 = dscr("v2", [S, G * D], BF16)
    qTs = dscr("qTs", [H * 128, MYST * ST], BF16)
    ctxT = dscr("ctxT", [H * 128, MYST * ST], BF16)
    x1pre = dscr("x1pre", [MYST * ST, HID], F32)
    x1row = dscr("x1row", [MYST * ST, HID], BF16)
    qrow = dscr("qrow", [HID, HID], BF16)
    qWT = dscr("qWT", [NKT * 128, HID], BF16)
    x2pre = dscr("x2pre", [MYST * ST, HID], F32)

    KB = 1024
    cid = _sb_at(nc, "cid", [128, 128], F32, 0)
    onescol = _sb_at(nc, "onescol", [128, 1], BF16, 512)
    onesrow_bf = _sb_at(nc, "onesrow_bf", [1, 128], BF16, 544)
    onesrow_f = _sb_at(nc, "onesrow_f", [1, 128], F32, 800)
    rows = _sb_at(nc, "rows", [1, 2048], F32, 1312)
    bqtA = _sb_at(nc, "bqtA", [128, 16], F32, 9504)
    bqtB = _sb_at(nc, "bqtB", [128, 16], F32, 9568)
    bktA = _sb_at(nc, "bktA", [128, 4], F32, 9632)
    bktB = _sb_at(nc, "bktB", [128, 4], F32, 9664)
    negUb = _sb_at(nc, "negUb", [128, 64], F32, 9696)
    dmask = _sb_at(nc, "dmask", [128, MYST * 2 * 128], BF16, 10176)
    bvb = _sb_at(nc, "bvb", [128, G * D], F32, 18368)
    sums_all = _sb_at(nc, "sums_all", [128, 64], F32, 22464)
    sq_all = _sb_at(nc, "sq_all", [128, 64], F32, 22720)
    stt1 = _sb_at(nc, "stt1", [128, 8], F32, 22976)
    stt2 = _sb_at(nc, "stt2", [128, 8], F32, 23008)
    stt3 = _sb_at(nc, "stt3", [128, 8], F32, 23040)
    stt4 = _sb_at(nc, "stt4", [128, 8], F32, 23072)
    BASE = 24 * KB

    qmaxrow = rows[0:1, 0:64]
    km2 = rows[0:1, 128:129]
    rtmp1 = rows[0:1, 256:320]
    rtmp2 = rows[0:1, 384:448]
    negu_row = rows[0:1, 512:576]
    rec_row = rows[0:1, 1024:1536]
    negu_bf = _sb_at(nc, "negu_bf", [1, 64], BF16, 23104)
    kmx_t = rows[0:1, 129:130]

    pA = es.enter_context(nc.psum_tensor("pA", [128, 1024], F32))
    pB = es.enter_context(nc.psum_tensor("pB", [128, 1024], F32))
    pC2 = es.enter_context(nc.psum_tensor("pC2", [128, 1024], F32))
    pD = es.enter_context(nc.psum_tensor("pD", [128, 512], F32))
    pE = es.enter_context(nc.psum_tensor("pE", [128, 512], F32))
    pCh = [pC2[:, 0:512], pC2[:, 512:1024]]
    pAh = [pA[:, 0:512], pA[:, 512:1024]]
    pBh = [pB[:, 0:512], pB[:, 512:1024]]
    p4rot = [pA[:, 0:512], pA[:, 512:1024], pB[:, 0:512], pB[:, 512:1024]]

    gI = p.sem("gI")
    tI = p.sem("tI")
    vI = p.sem("vI")

    def dma(eng, dst, src, sem, **kw):
        return p.do(eng, lambda e, d=dst, s=src: e.dma_start(d, s, **kw),
                    sem, 16)

    dma("G", cid[:], identf, gI)
    dma("G", onescol[:], onescol_i, gI)
    dma("G", onesrow_bf[:], onesrow_bf_i, gI)
    dma("G", onesrow_f[:], onesrow_f_i, gI)
    dma("G", dmask[:], dmask_i, gI)
    btmp = _sb_at(nc, "btmp", [128, 128], F32, BASE)
    dma("G", btmp[0:16, :], bqA_r, gI)
    dma("G", btmp[16:32, :], bqB_r, gI)
    dma("G", btmp[32:36, :], bkA_r, gI)
    dma("G", btmp[36:40, :], bkB_r, gI)
    dma("G", bvb[:], bass.AP(bv_i.tensor, 0, [[0, 128], [1, G * D]]), gI)
    g_consts = gI.n
    p.w("V", gI, g_consts)
    p.do("V", lambda e: e.memset(rows[:], 0.0), vI)
    p.do("V", lambda e: e.memset(negUb[:], 0.0), vI)
    p.w("T", gI, g_consts)
    p.w("T", vI, vI.n)
    p.do("T", lambda e: e.transpose(pE[:, 0:128], btmp[:], cid[:]), tI)
    p.w("V", tI, tI.n)
    p.do("V", lambda e: e.tensor_copy(bqtA[:], pE[:, 0:16]), vI)
    p.do("V", lambda e: e.tensor_copy(bqtB[:], pE[:, 16:32]), vI)
    p.do("V", lambda e: e.tensor_copy(bktA[:], pE[:, 32:36]), vI)
    p.do("V", lambda e: e.tensor_copy(bktB[:], pE[:, 36:40]), vI)
    v_binit = vI.n

    p.w("G", tI, tI.n)
    wk_sb_g = _sb_at(nc, "wk_sb", [128, NKT * 1024], BF16, BASE)
    for h2 in range(2):
        dma("G", wk_sb_g[:].rearrange("p (kt n) -> p kt n", n=1024)[:, h2 * 16:(h2 + 1) * 16, :],
            Wk.rearrange("(kt p) n -> p kt n", p=128)[:, h2 * 16:(h2 + 1) * 16, :], gI)
    wk_done_sem = gI
    wk_done_cnt = gI.n
    xbf_done = []
    for tc in range(8):
        dma("G", xbf[tc * 512:(tc + 1) * 512, :], x_full[tc * 512:(tc + 1) * 512, :], gI)
        xbf_done.append(gI.n)
    xbfm_done = []
    for sc in range(4):
        dma("G", xbfm[sc * 512:(sc + 1) * 512, :], x_mine[sc * 512:(sc + 1) * 512, :], gI)
        xbfm_done.append(gI.n)
    wv_bf = dscr("wv_bf", [128, NKT * 1024], BF16)
    for h2 in range(2):
        dma("G", wv_bf.rearrange("p (kt n) -> p kt n", n=1024)[:, h2 * 16:(h2 + 1) * 16, :],
            Wv.rearrange("(kt p) n -> p kt n", p=128)[:, h2 * 16:(h2 + 1) * 16, :], gI)
    wv_done_g = gI.n
    for nt in range(32):
        dma("G", wq_bf[nt * 128:(nt + 1) * 128, :].rearrange(
                "p (kt n) -> p kt n", n=128),
            Wq.rearrange("(kt p) (nt n) -> nt p kt n", p=128, n=128)[nt], gI)
    wqc_done_g = gI.n
    for nch in range(8):
        dma("G", wo_bf[nch * 128:(nch + 1) * 128, :].rearrange(
                "p (hd n) -> p hd n", n=512),
            Wo.rearrange("(hd p) (nch n) -> nch p hd n", p=128, n=512)[nch], gI)
    woc_done_g = gI.n
    wk_cast0 = gI.n

    if upto >= 1:
        t1 = p.sem("t1")
        v1 = p.sem("v1")
        s1 = p.sem("s1")
        q1 = p.sem("q1")
        g1 = p.sem("g1")
        wk_sb = wk_sb_g
        xt_sb = [_sb_at(nc, f"xt{i}", [128, NKT * 512], BF16, 88 * KB + i * 32 * KB)
                 for i in range(2)]
        tka_sb = [_sb_at(nc, f"tka{i}", [128, 512], F32, 152 * KB + i * 2 * KB) for i in range(2)]
        tkb_sb = [_sb_at(nc, f"tkb{i}", [128, 512], F32, 156 * KB + i * 2 * KB) for i in range(2)]
        klo_sb = [_sb_at(nc, f"klo{i}", [128, 512], BF16, 160 * KB + i * KB) for i in range(2)]
        khi_sb = [_sb_at(nc, f"khi{i}", [128, 512], BF16, 162 * KB + i * KB) for i in range(2)]
        ksqlo = [_sb_at(nc, f"ksqlo{i}", [128, 512], BF16, 164 * KB + i * KB) for i in range(2)]
        ksqhi = [_sb_at(nc, f"ksqhi{i}", [128, 512], BF16, 166 * KB + i * KB) for i in range(2)]
        ktmp1 = _sb_at(nc, "ktmp1", [128, 512], F32, 168 * KB)
        ktmp2 = _sb_at(nc, "ktmp2", [128, 512], F32, 170 * KB)


        xt_cons = [0, 0]
        bank_ev = [0, 0]
        norm_ev = [0, 0]
        ko_st = [0, 0]
        tr_done = []
        for tc in range(8):
            b = tc % 2
            for kt in range(NKT):
                eng = "Q"
                p.w(eng, gI, xbf_done[tc])
                if tc >= 2:
                    p.w(eng, t1, xt_cons[b])
                p.do(eng, lambda e, kt=kt, tc=tc, b=b: e.dma_start_transpose(
                    xt_sb[b][:, kt * 512:(kt + 1) * 512],
                    xbf[tc * 512:(tc + 1) * 512, kt * 128:(kt + 1) * 128]), q1, 16)
            tr_done.append(q1.n)
            dma("Q", tka_sb[b][:], tabKA[:, tc * 512:(tc + 1) * 512], q1)
            dma("Q", tkb_sb[b][:], tabKB[:, tc * 512:(tc + 1) * 512], q1)
            tab_done = q1.n
            for gp in range(4):
                pp = (tc * 4 + gp) % 2
                ph = [pAh, pBh][pp]
                kb = (tc * 4 + gp) % 2
                p.w("T", q1, tr_done[-1])
                p.w("T", wk_done_sem, wk_done_cnt)
                p.w("T", v1, bank_ev[pp])
                for half in range(2):
                    nt2 = gp * 2 + half
                    for kt in range(NKT):
                        n = p.do("T", lambda e, nt2=nt2, kt=kt, half=half, pp=pp, b=b:
                                 e.matmul(
                                     ph[half] if False else [pAh, pBh][pp][half],
                                     wk_sb[:, kt * 1024 + nt2 * 128:kt * 1024 + (nt2 + 1) * 128],
                                     xt_sb[b][:, kt * 512:(kt + 1) * 512],
                                     start=(kt == 0), stop=(kt == NKT - 1)),
                                 t1 if kt == NKT - 1 else None)
                        if kt == NKT - 1:
                            mm_done = n
                xt_cons[b] = mm_done
                p.w("V", t1, mm_done)
                p.w("V", q1, tab_done)
                p.w("V", g1, ko_st[kb])
                PA = [pAh, pBh][pp][0]
                PB = [pAh, pBh][pp][1]
                p.do("V", lambda e, gp=gp, PA=PA, b=b: e.scalar_tensor_tensor(
                    ktmp1[:], PA, bktA[:, gp:gp + 1], tka_sb[b][:], ALU.add, ALU.mult))
                p.do("V", lambda e, gp=gp, PB=PB, b=b: e.scalar_tensor_tensor(
                    ktmp2[:], PB, bktB[:, gp:gp + 1], tkb_sb[b][:], ALU.add, ALU.mult))
                p.do("V", lambda e, kb=kb: e.tensor_sub(klo_sb[kb][:], ktmp1[:], ktmp2[:]))
                p.do("V", lambda e, gp=gp, PA=PA, b=b: e.scalar_tensor_tensor(
                    ktmp1[:], PA, bktA[:, gp:gp + 1], tkb_sb[b][:], ALU.add, ALU.mult))
                p.do("V", lambda e, gp=gp, PB=PB, b=b: e.scalar_tensor_tensor(
                    ktmp2[:], PB, bktB[:, gp:gp + 1], tka_sb[b][:], ALU.add, ALU.mult))
                bank_ev[pp] = p.do("V", lambda e, kb=kb: e.tensor_add(
                    khi_sb[kb][:], ktmp1[:], ktmp2[:]), v1)
                ev_done = bank_ev[pp]
                p.w("S", v1, ev_done)
                p.do("S", lambda e, kb=kb: e.activation(ksqlo[kb][:], klo_sb[kb][:], AF.Square))
                sq_done = p.do("S", lambda e, kb=kb: e.activation(
                    ksqhi[kb][:], khi_sb[kb][:], AF.Square), s1)
                nb = [pD, pE][gp % 2]
                p.w("T", s1, sq_done)
                p.w("T", v1, norm_ev[gp % 2])
                p.do("T", lambda e, kb=kb, nb=nb: e.matmul(
                    nb[0:1, :], onescol[:], ksqlo[kb][:], start=True, stop=False,
                    skip_group_check=True))
                ones_done = p.do("T", lambda e, kb=kb, nb=nb: e.matmul(
                    nb[0:1, :], onescol[:], ksqhi[kb][:], start=False, stop=True,
                    skip_group_check=True), t1)
                p.w("V", t1, ones_done)
                p.w("V", vI, v_binit)
                rmx = p.do("V", lambda e, nb=nb: e.reduce_max(kmx_t, nb[0:1, :], axis=AX.X), v1)
                p.w("V", v1, rmx)
                norm_ev[gp % 2] = p.do("V", lambda e: e.tensor_max(km2, km2, kmx_t), v1)
                p.w("V", v1, norm_ev[gp % 2])
                p.w("Q", v1, ev_done)
                for (buf, hoff) in ((klo_sb, 0), (khi_sb, 64)):
                    for two in range(2):
                        g = 2 * gp + two
                        ko_st[kb] = 16 + g1.n
                        dma("Q", kT[g * 128 + hoff:g * 128 + hoff + 64,
                                   tc * 512:(tc + 1) * 512],
                            buf[kb][two * 64:(two + 1) * 64, :], g1)
        p1a_g_final = g1.n
        p1a_km_final = v1.n

    if upto >= 2:
        p.barrier()
        t2 = p.sem("t2")
        v2s = p.sem("v2s")
        q2 = p.sem("q2")
        g2 = p.sem("g2")
        wv_sb = _sb_at(nc, "wv_sbp", [128, NKT * 1024], BF16, BASE)
        vbuf = [_sb_at(nc, f"vbuf{i}", [128, 512], BF16, 152 * KB + i * KB) for i in range(2)]
        p.w("Q", gI, wv_done_g)
        p.w("Q", t1, t1.n)
        dma("Q", wv_sb[:], wv_bf, q2)
        wv_done = q2.n
        xt_cons = [0, 0]
        bank_ev = [0, 0]
        vb_st = [0, 0]
        for tc in range(8):
            b = tc % 2
            for kt in range(NKT):
                eng = "Q"
                p.w(eng, gI, xbf_done[tc])
                if upto >= 1 and tc < 2:
                    p.w(eng, t1, t1.n)
                if tc >= 2:
                    p.w(eng, t2, xt_cons[b])
                p.do(eng, lambda e, kt=kt, tc=tc, b=b: e.dma_start_transpose(
                    xt_sb[b][:, kt * 512:(kt + 1) * 512],
                    xbf[tc * 512:(tc + 1) * 512, kt * 128:(kt + 1) * 128]), q2, 16)
            trd = q2.n
            for ts in range(4):
                for nch in range(2):
                    i2 = (tc * 8 + ts * 2 + nch) % 2
                    p.w("T", q2, max(trd, wv_done))
                    p.w("T", v2s, bank_ev[i2])
                    for kt in range(NKT):
                        n = p.do("T", lambda e, ts=ts, nch=nch, kt=kt, i2=i2, b=b: e.matmul(
                            pAh[i2],
                            xt_sb[b][:, kt * 512 + ts * 128:kt * 512 + (ts + 1) * 128],
                            wv_sb[:, kt * 1024 + nch * 512:kt * 1024 + (nch + 1) * 512],
                            start=(kt == 0), stop=(kt == NKT - 1)),
                            t2 if kt == NKT - 1 else None)
                        if kt == NKT - 1:
                            mm_done = n
                    xt_cons[b] = mm_done
                    p.w("V", t2, mm_done)
                    p.w("V", g2, vb_st[i2])
                    bank_ev[i2] = p.do("V", lambda e, nch=nch, i2=i2: e.tensor_add(
                        vbuf[i2][:], pAh[i2], bvb[:, nch * 512:(nch + 1) * 512]), v2s)
                    p.w("Q", v2s, bank_ev[i2])
                    vb_st[i2] = 16 + g2.n
                    dma("Q", v2[tc * 512 + ts * 128:tc * 512 + (ts + 1) * 128,
                               nch * 512:(nch + 1) * 512], vbuf[i2][:], g2)
        p1b_g_final = g2.n

    if upto >= 3:
        p.barrier()
        t3 = p.sem("t3")
        v3 = p.sem("v3")
        s3 = p.sem("s3")
        q3 = p.sem("q3")
        g3 = p.sem("g3")
        xtm_sb = [_sb_at(nc, f"xtm{i}", [128, NKT * 512], BF16, BASE + i * 32 * KB)
                  for i in range(2)]
        wq_sb = [_sb_at(nc, f"wqsb{i}", [128, NKT * 128], BF16, 88 * KB + i * 8 * KB)
                 for i in range(4)]
        tqa_sb = [_sb_at(nc, f"tqa{i}", [128, 512], F32, 120 * KB + i * 2 * KB) for i in range(2)]
        tqb_sb = [_sb_at(nc, f"tqb{i}", [128, 512], F32, 124 * KB + i * 2 * KB) for i in range(2)]
        qtmp1 = _sb_at(nc, "qtmp1", [128, 512], F32, 128 * KB)
        qtmp2 = _sb_at(nc, "qtmp2", [128, 512], F32, 130 * KB)
        qlo_sb = [_sb_at(nc, f"qlo{i}", [128, 512], BF16, 132 * KB + i * KB) for i in range(2)]
        qhi_sb = [_sb_at(nc, f"qhi{i}", [128, 512], BF16, 134 * KB + i * KB) for i in range(2)]
        qsqlo = [_sb_at(nc, f"qsqlo{i}", [128, 512], BF16, 136 * KB + i * KB) for i in range(2)]
        qsqhi = [_sb_at(nc, f"qsqhi{i}", [128, 512], BF16, 138 * KB + i * KB) for i in range(2)]
        wqc_done = wqc_done_g
        xtm_cons = [0, 0]
        wq_cons = [0, 0, 0, 0]
        bank_ev = [0, 0]
        norm_ev = [0, 0]
        qo_st = [0, 0]
        for sc in range(4):
            b = sc % 2
            for kt in range(NKT):
                eng = "Q"
                p.w(eng, gI, xbfm_done[sc])
                if upto >= 2 and sc < 2:
                    p.w(eng, t2, t2.n)
                if sc >= 2:
                    p.w(eng, t3, xtm_cons[b])
                p.do(eng, lambda e, kt=kt, sc=sc, b=b: e.dma_start_transpose(
                    xtm_sb[b][:, kt * 512:(kt + 1) * 512],
                    xbfm[sc * 512:(sc + 1) * 512, kt * 128:(kt + 1) * 128]), q3, 16)
            trd = q3.n
            dma("Q", tqa_sb[b][:], tabQA[:, sc * 512:(sc + 1) * 512], q3)
            dma("Q", tqb_sb[b][:], tabQB[:, sc * 512:(sc + 1) * 512], q3)
            tabd = q3.n
            for hp in range(16):
                wbs = [(hp % 2) * 2, (hp % 2) * 2 + 1]
                for half in range(2):
                    nt2 = 2 * hp + half
                    p.w("Q", gI, wqc_done)
                    p.w("Q", t3, wq_cons[wbs[half]])
                    dma("Q", wq_sb[wbs[half]][:], wq_bf[nt2 * 128:(nt2 + 1) * 128, :], q3)
                wqld = q3.n
                pp = hp % 2
                p.w("T", q3, max(trd, wqld))
                p.w("T", v3, bank_ev[pp])
                for half in range(2):
                    for kt in range(NKT):
                        n = p.do("T", lambda e, kt=kt, half=half, pp=pp, b=b, wbs=wbs:
                                 e.matmul(
                                     [pAh, pBh][pp][half],
                                     wq_sb[wbs[half]][:, kt * 128:(kt + 1) * 128],
                                     xtm_sb[b][:, kt * 512:(kt + 1) * 512],
                                     start=(kt == 0), stop=(kt == NKT - 1)),
                                 t3 if kt == NKT - 1 else None)
                        if kt == NKT - 1:
                            mm_done = n
                xtm_cons[b] = mm_done
                wq_cons[wbs[0]] = mm_done
                wq_cons[wbs[1]] = mm_done
                qb = hp % 2
                PA = [pAh, pBh][pp][0]
                PB = [pAh, pBh][pp][1]
                p.w("V", t3, mm_done)
                p.w("V", q3, tabd)
                p.w("V", g3, qo_st[qb])
                p.do("V", lambda e, hp=hp, PA=PA, b=b: e.scalar_tensor_tensor(
                    qtmp1[:], PA, bqtA[:, hp:hp + 1], tqa_sb[b][:], ALU.add, ALU.mult))
                p.do("V", lambda e, hp=hp, PB=PB, b=b: e.scalar_tensor_tensor(
                    qtmp2[:], PB, bqtB[:, hp:hp + 1], tqb_sb[b][:], ALU.add, ALU.mult))
                p.do("V", lambda e, qb=qb: e.tensor_sub(qlo_sb[qb][:], qtmp1[:], qtmp2[:]))
                p.do("V", lambda e, hp=hp, PA=PA, b=b: e.scalar_tensor_tensor(
                    qtmp1[:], PA, bqtA[:, hp:hp + 1], tqb_sb[b][:], ALU.add, ALU.mult))
                p.do("V", lambda e, hp=hp, PB=PB, b=b: e.scalar_tensor_tensor(
                    qtmp2[:], PB, bqtB[:, hp:hp + 1], tqa_sb[b][:], ALU.add, ALU.mult))
                bank_ev[pp] = p.do("V", lambda e, qb=qb: e.tensor_add(
                    qhi_sb[qb][:], qtmp1[:], qtmp2[:]), v3)
                evd = bank_ev[pp]
                p.w("S", v3, evd)
                p.do("S", lambda e, qb=qb: e.activation(qsqlo[qb][:], qlo_sb[qb][:], AF.Square))
                sqd = p.do("S", lambda e, qb=qb: e.activation(
                    qsqhi[qb][:], qhi_sb[qb][:], AF.Square), s3)
                nb = [pD, pE][hp % 2]
                p.w("T", s3, sqd)
                p.w("T", v3, norm_ev[hp % 2])
                p.do("T", lambda e, qb=qb, nb=nb: e.matmul(
                    nb[0:1, :], onescol[:], qsqlo[qb][:], start=True, stop=False,
                    skip_group_check=True))
                od = p.do("T", lambda e, qb=qb, nb=nb: e.matmul(
                    nb[0:1, :], onescol[:], qsqhi[qb][:], start=False, stop=True,
                    skip_group_check=True), t3)
                p.w("V", t3, od)
                norm_ev[hp % 2] = p.do("V", lambda e, nb=nb, sc=sc, hp=hp: e.reduce_max(
                    qmaxrow[:, sc * 16 + hp:sc * 16 + hp + 1], nb[0:1, :], axis=AX.X), v3)
                p.w("Q", v3, evd)
                for (buf, hoff) in ((qlo_sb, 0), (qhi_sb, 64)):
                    for two in range(2):
                        hh = 2 * hp + two
                        qo_st[qb] = 16 + g3.n
                        dma("Q", qTs[hh * 128 + hoff:hh * 128 + hoff + 64,
                                    sc * 512:(sc + 1) * 512],
                            buf[qb][two * 64:(two + 1) * 64, :], g3)
        p2_v_final = v3.n
        p2_g_final = g3.n

    if upto >= 5:
        q6 = p.sem("q6")
        v6 = p.sem("v6")
        g6 = p.sem("g6")
        d6 = p.sem("d6")
        s6b = p.sem("s6b")
        wf_sb = [_sb_at(nc, f"wfsb{i}", [128, 4096], F32, 128 * KB + i * 16 * KB)
                 for i in range(2)]
        qf_sb = _sb_at(nc, "qf_sb", [128, 4096], BF16, 160 * KB)
        btmp6 = _sb_at(nc, "btmp6", [128, 4096], BF16, 168 * KB)
        strip6 = [_sb_at(nc, f"strip6{i}", [128, 4096], BF16, 176 * KB + i * 8 * KB)
                  for i in range(2)]
        absb6 = _sb_at(nc, "absb6", [128, 4096], F32, 176 * KB)
        wf_cons = [0, 0]
        qf_cons = [0]
        strip_cons = [0, 0]

        def p6a(ot):
            wb = ot % 2

            def emit():
                p.w("Q", v6, wf_cons[wb])
                dma("Q", wf_sb[wb][:], Wf[ot * 128:(ot + 1) * 128, :], q6)
                ld = q6.n
                p.w("V", q6, ld)
                p.w("V", g6, qf_cons[0])
                p.do("V", lambda e, wb=wb: e.tensor_scalar_mul(
                    absb6[:], wf_sb[wb][:], -1.0))
                mx = p.do("V", lambda e, wb=wb: e.tensor_max(
                    absb6[:], absb6[:], wf_sb[wb][:]), v6)
                p.w("V", v6, mx)
                rd = p.do("V", lambda e: e.tensor_reduce(
                    stt1[:, 0:1], absb6[:], AX.X, ALU.add), v6)
                p.w("V", v6, rd)
                KSC = 4096.0 / 0.7
                sum_b = stt1[:, 0:1].broadcast_to([128, 4096])
                p.do("V", lambda e, wb=wb, sum_b=sum_b: e.scalar_tensor_tensor(
                    qf_sb[:], wf_sb[wb][:], KSC, sum_b, ALU.mult, ALU.is_gt))
                p.do("V", lambda e, wb=wb, sum_b=sum_b: e.scalar_tensor_tensor(
                    btmp6[:], wf_sb[wb][:], -KSC, sum_b, ALU.mult, ALU.is_gt))
                wf_cons[wb] = p.do("V", lambda e: e.tensor_sub(qf_sb[:], qf_sb[:], btmp6[:]), v6)
                p.w("G", v6, wf_cons[wb])
                qf_cons[0] = 16 + g6.n
                dma("G", qrow[ot * 128:(ot + 1) * 128, :], qf_sb[:], g6)
            return emit

        def p6b(kt):
            sb_ = kt % 2

            def emit():
                p.w("Q", g6, 32 * 16)
                p.w("Q", s6b, strip_cons[sb_])
                p.do("Q", lambda e, kt=kt, sb_=sb_: e.dma_start_transpose(
                    strip6[sb_][:], qrow[:, kt * 128:(kt + 1) * 128]), d6, 16)
                trd = d6.n
                p.w("G", d6, trd)
                strip_cons[sb_] = 16 + s6b.n
                dma("G", qWT[kt * 128:(kt + 1) * 128, :], strip6[sb_][:], s6b)
            return emit

    if upto >= 4:
        p.barrier()
        t4 = p.sem("t4")
        v4 = p.sem("v4")
        s4 = p.sem("s4")
        q4 = p.sem("q4")
        g4 = p.sem("g4")
        qc_sb = [_sb_at(nc, f"qc{i}", [128, H * 512], BF16, BASE + i * 32 * KB)
                 for i in range(2)]
        kg_sb = [_sb_at(nc, f"kg{i}", [128, S], BF16, 88 * KB + i * 8 * KB) for i in range(2)]
        vg_sb = [_sb_at(nc, f"vg{i}", [128, S], BF16, 104 * KB + i * 8 * KB) for i in range(2)]
        pt_sb = [_sb_at(nc, f"pt{i}", [128, 1024], BF16, 120 * KB + i * 2 * KB) for i in range(2)]
        recb = _sb_at(nc, "recb", [128, 512], F32, 124 * KB)
        ctx_sb = [_sb_at(nc, f"ctx{i}", [128, 512], BF16, 126 * KB + i * KB) for i in range(2)]

        p.w("V", v3, p2_v_final)
        p.w("V", v1, p1a_km_final)
        vnu = p.do("V", lambda e: e.tensor_scalar(
            rtmp1, qmaxrow, km2, None, ALU.mult), v4)
        p.w("V", v4, vnu)
        first = True
        for j in (0, 2, 4, 6, 8, 10):
            c = float(2 ** j)
            n_ = p.do("V", lambda e, c=c: e.tensor_scalar(
                rtmp2, rtmp1, 0.5 / c, 0.5 * c, ALU.mult, ALU.add), v4)
            p.w("V", v4, n_)
            if first:
                n_ = p.do("V", lambda e: e.tensor_copy(negu_row, rtmp2), v4)
                first = False
            else:
                n_ = p.do("V", lambda e: e.tensor_tensor(
                    negu_row, negu_row, rtmp2, ALU.min), v4)
            p.w("V", v4, n_)
        nr = p.do("V", lambda e: e.tensor_scalar(
            negu_row, negu_row, -1.01, -1.0, ALU.mult, ALU.add), v4)
        p.w("V", v4, nr)
        vnu2 = p.do("V", lambda e: e.tensor_copy(negu_bf[:], negu_row), v4)
        p.w("T", v4, vnu2)
        tnu = p.do("T", lambda e: e.matmul(
            pE[:, 0:64], onesrow_bf[:], negu_bf[:], start=True, stop=True), t4)
        p.w("S", t4, tnu)
        negub_done = p.do("S", lambda e: e.copy(negUb[:], pE[:, 0:64]), s4)

        p6a_list = [p6a(ot) for ot in range(32)] if upto >= 5 else []
        p6b_list = [p6b(kt) for kt in range(NKT)] if upto >= 5 else []
        p6a_i = 0
        p6b_i = 0

        qc_cons = [0, 0]
        kg_cons = [0, 0]
        vg_cons = [0, 0]
        pt_cons = [0, 0]
        pt_mask_req = [None, None]
        sc_ev = [0, 0]
        ctx_ev = [0, 0]
        sum_ev = 0
        rec_ev = 0
        ctx_st = [0, 0]
        qrow_targets = []

        for c in range(4):
            Tc = (8 * c + 8) * 128
            npair = 4 * c + 4
            cpar = c % 2
            p.w("Q", t4, qc_cons[cpar])
            p.w("Q", g3 if upto >= 3 else gI, p2_g_final if upto >= 3 else 0)
            dma("Q", qc_sb[cpar][:].rearrange("p (h s) -> p h s", s=512),
                qTs.rearrange("(h p) s -> p h s", p=128)[:, :, c * 512:(c + 1) * 512], q4)
            qc_ld = q4.n
            for g in range(G):
                gb = g % 2
                p.w("Q", t4, kg_cons[gb])
                p.w("Q", g1 if upto >= 1 else gI, p1a_g_final if upto >= 1 else 0)
                dma("Q", kg_sb[gb][:, 0:Tc], kT[g * 128:(g + 1) * 128, 0:Tc], q4)
                p.w("Q", t4, vg_cons[gb])
                p.w("Q", g2 if upto >= 2 else gI, p1b_g_final if upto >= 2 else 0)
                dma("Q", vg_sb[gb][:].rearrange("p (tt d) -> p tt d", d=128)[:, 0:Tc // 128, :],
                    v2[0:Tc, g * 128:(g + 1) * 128].rearrange("(tt t) d -> t tt d", t=128),
                    q4)
                kv_ld = q4.n
                for r in range(4):
                    h = g * 4 + r
                    cr = (g * 4 + r) % 2

                    def emit_scores(pr, gb=gb, h=h, cpar=cpar):
                        s0 = 128 * max(0, pr - 4 * c)
                        pb = pr % 2
                        p.w("T", q4, max(qc_ld, kv_ld))
                        p.w("T", s4, sc_ev[pb])
                        p.w("T", s4, negub_done)
                        p.do("T", lambda e, pr=pr, s0=s0, gb=gb, h=h, pb=pb, cpar=cpar:
                             e.matmul(
                                 [pA, pB][pb][:, s0:512],
                                 kg_sb[gb][:, (2 * pr) * 128:(2 * pr + 1) * 128],
                                 qc_sb[cpar][:, h * 512 + s0:h * 512 + 512],
                                 start=True, stop=True))
                        scd = p.do("T", lambda e, pr=pr, s0=s0, gb=gb, h=h, pb=pb,
                                   cpar=cpar: e.matmul(
                                       [pA, pB][pb][:, 512 + s0:1024],
                                       kg_sb[gb][:, (2 * pr + 1) * 128:(2 * pr + 2) * 128],
                                       qc_sb[cpar][:, h * 512 + s0:h * 512 + 512],
                                       start=True, stop=True), t4)
                        kg_cons[gb] = scd
                        qc_cons[cpar] = scd
                        return scd

                    scd_next = emit_scores(0)
                    for pr in range(npair):
                        l0 = max(0, pr - 4 * c)
                        s0 = 128 * l0
                        pb = pr % 2
                        scd = scd_next
                        p.w("S", t4, scd)
                        if pt_cons[pb]:
                            p.w("S", t4, pt_cons[pb])
                        if pt_mask_req[pb] is not None:
                            p.w("S", v4, pt_mask_req[pb])
                        expd = p.do("S", lambda e, pb=pb, s0=s0, c=c, h=h: e.activation(
                            pt_sb[pb][:].rearrange("p (two x) -> p two x", two=2)[:, :, s0:512],
                            [pA, pB][pb][:].rearrange("p (two x) -> p two x", two=2)[:, :, s0:512],
                            AF.Exp, bias=negUb[:, c * 16 + h // 2:c * 16 + h // 2 + 1],
                            scale=1.0), s4)
                        sc_ev[pb] = expd
                        masked = (4 * c <= pr <= 4 * c + 3)
                        if masked:
                            l = pr - 4 * c
                            p.w("V", s4, expd)
                            p.do("V", lambda e, pb=pb, l=l, pr=pr: e.tensor_mul(
                                pt_sb[pb][:, l * 128:(l + 1) * 128],
                                pt_sb[pb][:, l * 128:(l + 1) * 128],
                                dmask[:, (pr * 2) * 128:(pr * 2 + 1) * 128]))
                            mkd = p.do("V", lambda e, pb=pb, l=l, pr=pr: e.tensor_mul(
                                pt_sb[pb][:, 512 + l * 128:512 + (l + 1) * 128],
                                pt_sb[pb][:, 512 + l * 128:512 + (l + 1) * 128],
                                dmask[:, (pr * 2 + 1) * 128:(pr * 2 + 2) * 128]), v4)
                            pt_mask_req[pb] = mkd
                        else:
                            pt_mask_req[pb] = None
                        if pr + 1 < npair:
                            scd_next = emit_scores(pr + 1)
                        p.w("T", s4, expd)
                        if masked:
                            p.w("T", v4, mkd)
                        if pr == 0:
                            p.w("T", v4, ctx_ev[cr])
                            p.w("T", v4, sum_ev)
                        p.do("T", lambda e, pr=pr, s0=s0, gb=gb, pb=pb, cr=cr: e.matmul(
                            pCh[cr][:, s0:512],
                            vg_sb[gb][:, (2 * pr) * 128:(2 * pr + 1) * 128],
                            pt_sb[pb][:, s0:512], start=(pr == 0), stop=False,
                            skip_group_check=True))
                        pvd = p.do("T", lambda e, pr=pr, s0=s0, gb=gb, pb=pb, cr=cr:
                                   e.matmul(
                                       pCh[cr][:, s0:512],
                                       vg_sb[gb][:, (2 * pr + 1) * 128:(2 * pr + 2) * 128],
                                       pt_sb[pb][:, 512 + s0:1024], start=False,
                                       stop=(pr == npair - 1), skip_group_check=True), t4)
                        vg_cons[gb] = pvd
                        p.do("T", lambda e, pr=pr, s0=s0, pb=pb: e.matmul(
                            pD[0:1, s0:512], onescol[:], pt_sb[pb][:, s0:512],
                            start=(pr == 0), stop=False, skip_group_check=True))
                        oned = p.do("T", lambda e, pr=pr, s0=s0, pb=pb: e.matmul(
                            pD[0:1, s0:512], onescol[:], pt_sb[pb][:, 512 + s0:1024],
                            start=False, stop=(pr == npair - 1),
                            skip_group_check=True), t4)
                        pt_cons[pb] = oned
                    p.w("V", t4, pt_cons[0])
                    p.w("V", t4, pt_cons[1])
                    sum_ev = p.do("V", lambda e: e.reciprocal(rec_row, pD[0:1, :]), v4)
                    p.w("T", v4, max(sum_ev, rec_ev))
                    bcd = p.do("T", lambda e: e.matmul(
                        pE[:], onesrow_f[:], rec_row, start=True, stop=True), t4)
                    cb = (g * 4 + r) % 2
                    p.w("V", t4, bcd)
                    p.w("V", g4, ctx_st[cb])
                    rec_ev = p.do("V", lambda e: e.tensor_copy(recb[:], pE[:]), v4)
                    ctx_ev[cr] = p.do("V", lambda e, cb=cb, cr=cr: e.tensor_mul(
                        ctx_sb[cb][:], pCh[cr][:], recb[:]), v4)
                    p.w("G", v4, ctx_ev[cr])
                    ctx_st[cb] = 16 + g4.n
                    dma("G", ctxT[h * 128:(h + 1) * 128, c * 512:(c + 1) * 512],
                        ctx_sb[cb][:], g4)
                it = c * 8 + g
                if upto >= 5:
                    if c < 3:
                        want = min(32, round((it + 1) * 32 / 24))
                        while p6a_i < want:
                            qrow_targets.append(p6a_list[p6a_i]())
                            p6a_i += 1
                    else:
                        while p6a_i < 32:
                            qrow_targets.append(p6a_list[p6a_i]())
                            p6a_i += 1
                        want_b = min(NKT, (g + 1) * 4)
                        while p6b_i < want_b:
                            p6b_list[p6b_i]()
                            p6b_i += 1
        p3_g_final = g4.n
        if upto >= 5:
            p6_d_final = s6b.n

    if upto >= 6:
        p.barrier()
        t5 = p.sem("t5")
        v5 = p.sem("v5")
        s5 = p.sem("s5")
        q5 = p.sem("q5")
        g5 = p.sem("g5")
        ctxh_sb = _sb_at(nc, "ctxh", [128, H * 1024], BF16, BASE)
        wo_sb = [_sb_at(nc, f"wosb{i}", [128, NKT * 512], BF16, 88 * KB + i * 32 * KB)
                 for i in range(2)]
        bo_sl = [_sb_at(nc, f"bosl{i}", [128, 512], F32, 152 * KB + i * 2 * KB) for i in range(2)]
        xm_sl = [_sb_at(nc, f"xmsl{i}", [128, 512], F32, 156 * KB + i * 2 * KB) for i in range(2)]
        tmpf4 = _sb_at(nc, "tmpf4", [128, 512], F32, 160 * KB)
        x1c_sb = [_sb_at(nc, f"x1c{i}", [128, 512], F32, 162 * KB + i * 2 * KB) for i in range(2)]
        junkb = _sb_at(nc, "junkb4", [128, 4096], BF16, 166 * KB)
        xp_full = _sb_at(nc, "xp4", [128, 4096], F32, 174 * KB)
        xr_full = _sb_at(nc, "xr4", [128, 4096], BF16, 190 * KB)
        woc_done = woc_done_g

        def ln_stripe5(st8, grp, gdg):
            st = grp * 8 + st8
            nonlocal xp_cons, xr_st
            p.w("Q", v5, xp_cons)
            p.w("Q", g5, gdg)
            dma("Q", xp_full[:], x1pre[st * 128:(st + 1) * 128, :], q5)
            xld = q5.n
            p.w("V", q5, xld)
            n_ = p.do("V", lambda e: e.tensor_reduce(
                stt1[:, 0:1], xp_full[:], AX.X, ALU.add), v5)
            p.w("V", v5, n_)
            n_ = p.do("V", lambda e: e.scalar_tensor_tensor(
                junkb[:], xp_full[:], 1.0, xp_full[:], ALU.mult, ALU.mult,
                accum_out=stt1[:, 1:2]), v5)
            p.w("V", v5, n_)
            n_ = p.do("V", lambda e: e.tensor_scalar_mul(
                stt1[:, 2:3], stt1[:, 0:1], 1.0 / 4096.0), v5)
            p.w("V", v5, n_)
            n_ = p.do("V", lambda e: e.tensor_scalar_mul(
                stt1[:, 3:4], stt1[:, 1:2], 1.0 / 4096.0), v5)
            p.w("V", v5, n_)
            n_ = p.do("V", lambda e: e.tensor_mul(
                stt2[:, 0:1], stt1[:, 2:3], stt1[:, 2:3]), v5)
            p.w("V", v5, n_)
            n_ = p.do("V", lambda e: e.tensor_sub(
                stt2[:, 1:2], stt1[:, 3:4], stt2[:, 0:1]), v5)
            p.w("V", v5, n_)
            n_ = p.do("V", lambda e: e.tensor_scalar_add(
                stt2[:, 1:2], stt2[:, 1:2], EPS), v5)
            p.w("V", v5, n_)
            n_ = p.do("V", lambda e: e.reciprocal(stt2[:, 2:3], stt2[:, 1:2]), v5)
            p.w("V", v5, n_)
            n_ = p.do("V", lambda e: e.tensor_scalar_min(
                stt2[:, 3:4], stt2[:, 2:3], 1.0), v5)
            p.w("V", v5, n_)
            for _it in range(12):
                n_ = p.do("V", lambda e: e.tensor_mul(
                    stt3[:, 0:1], stt2[:, 3:4], stt2[:, 3:4]), v5)
                p.w("V", v5, n_)
                n_ = p.do("V", lambda e: e.tensor_mul(
                    stt3[:, 0:1], stt3[:, 0:1], stt2[:, 1:2]), v5)
                p.w("V", v5, n_)
                n_ = p.do("V", lambda e: e.tensor_scalar(
                    stt3[:, 0:1], stt3[:, 0:1], -0.5, 1.5, ALU.mult, ALU.add), v5)
                p.w("V", v5, n_)
                n_ = p.do("V", lambda e: e.tensor_mul(
                    stt2[:, 3:4], stt2[:, 3:4], stt3[:, 0:1]), v5)
                p.w("V", v5, n_)
            p.w("V", g5, xr_st)
            xp_cons = p.do("V", lambda e: e.tensor_scalar(
                xr_full[:], xp_full[:], stt1[:, 2:3], stt2[:, 3:4],
                ALU.subtract, ALU.mult), v5)
            p.w("G", v5, xp_cons)
            xr_st = 16 + g5.n
            dma("G",
                x1row[st * 128:(st + 1) * 128, :], xr_full[:], g5)
        pending_ln = []
        g5_x1row = [0, 0]
        ctxh_cons = 0
        wo_cons = [0, 0]
        bank_ev4 = [0, 0, 0, 0]
        bo_cons = [0, 0]
        xm_cons = [0, 0]
        x1c_st = [0, 0]
        xp_cons = 0
        xr_st = 0
        for grp in range(2):
            p.w("Q", t5, ctxh_cons)
            p.w("Q", g4, p3_g_final)
            dma("Q", ctxh_sb[:].rearrange("p (h s) -> p h s", s=1024),
                ctxT.rearrange("(h p) s -> p h s", p=128)[:, :, grp * 1024:(grp + 1) * 1024],
                q5)
            ctxh_ld = q5.n
            for nch in range(8):
                wb = nch % 2
                p.w("Q", gI, woc_done)
                p.w("Q", t5, wo_cons[wb])
                dma("Q", wo_sb[wb][:], wo_bf[nch * 128:(nch + 1) * 128, :], q5)
                wold = q5.n
                p.w("G", v5, bo_cons[wb])
                dma("G", bo_sl[wb][:], bass.AP(
                    bo_i.tensor, nch * 512, [[0, 128], [1, 512]]), g5)
                bold = g5.n
                for st8 in range(8):
                    st = grp * 8 + st8
                    i4 = (nch * 8 + st8) % 4
                    ldb = (nch * 8 + st8) % 2
                    p.w("Q", q5, 0)
                    p.w("Q", v5, xm_cons[ldb])
                    dma("Q", xm_sl[ldb][:],
                        x_mine[st * 128:(st + 1) * 128, nch * 512:(nch + 1) * 512], q5)
                    xmld = q5.n
                    p.w("T", q5, max(ctxh_ld, wold))
                    p.w("T", v5, bank_ev4[i4])
                    for hd in range(32):
                        n = p.do("T", lambda e, hd=hd, st8=st8, wb=wb, i4=i4: e.matmul(
                            p4rot[i4],
                            ctxh_sb[:, hd * 1024 + st8 * 128:hd * 1024 + (st8 + 1) * 128],
                            wo_sb[wb][:, hd * 512:(hd + 1) * 512],
                            start=(hd == 0), stop=(hd == 31)),
                            t5 if hd == 31 else None)
                        if hd == 31:
                            mmd = n
                    ctxh_cons = mmd
                    wo_cons[wb] = mmd
                    xc = (nch * 8 + st8) % 2
                    p.w("V", t5, mmd)
                    p.w("V", g5, bold)
                    p.w("V", q5, xmld)
                    p.w("V", g5, x1c_st[xc])
                    p.do("V", lambda e, i4=i4, wb=wb: e.tensor_add(
                        tmpf4[:], p4rot[i4], bo_sl[wb][:]))
                    ev = p.do("V", lambda e, ldb=ldb, xc=xc:
                              e.scalar_tensor_tensor(
                                  x1c_sb[xc][:], tmpf4[:], 0.0, xm_sl[ldb][:],
                                  ALU.add, ALU.add),
                              v5)
                    bank_ev4[i4] = ev
                    xm_cons[ldb] = ev
                    bo_cons[wb] = ev
                    p.w("G", v5, ev)
                    x1c_st[xc] = 16 + g5.n
                    dma("G", x1pre[st * 128:(st + 1) * 128, nch * 512:(nch + 1) * 512],
                        x1c_sb[xc][:], g5)
                    if pending_ln and st8 == 7:
                        ln_stripe5(*pending_ln.pop(0))
                        if not pending_ln:
                            g5_x1row[0] = g5.n
            grp_done_v = v5.n
            grp_done_s = s5.n
            grp_done_g = g5.n
            if grp == 0:
                pending_ln = [(st8, grp, grp_done_g) for st8 in range(8)]
            else:
                for args in pending_ln:
                    ln_stripe5(*args)
                pending_ln = []
                for st8 in range(8):
                    ln_stripe5(st8, 1, grp_done_g)
        p45_g_final = g5.n
        g5_x1row[1] = g5.n

    if upto >= 7:
        t5_final = t5.n
        v5_final = v5.n
        g5_final = g5.n
        t7 = p.sem("t7")
        v7 = p.sem("v7")
        s7 = p.sem("s7")
        q7 = p.sem("q7")
        g7 = p.sem("g7")
        q8 = p.sem("q8")
        x1T_sb = _sb_at(nc, "x1T", [128, NKT * 1024], BF16, BASE)
        qw_sb = [_sb_at(nc, f"qwsb{i}", [128, NKT * 512], BF16, 88 * KB + i * 32 * KB)
                 for i in range(2)]
        fb_sl = [_sb_at(nc, f"fbsl{i}", [128, 512], F32, 152 * KB + i * 2 * KB) for i in range(2)]
        x1s_sl = [_sb_at(nc, f"x1ssl{i}", [128, 512], BF16, 156 * KB + i * KB) for i in range(2)]
        tmpf7 = _sb_at(nc, "tmpf7", [128, 512], F32, 158 * KB)
        x2c_sb = [_sb_at(nc, f"x2c{i}", [128, 512], F32, 160 * KB + i * 2 * KB) for i in range(2)]
        junkb7 = _sb_at(nc, "junkb7", [128, 4096], BF16, 164 * KB)
        xp_full7 = _sb_at(nc, "xp7", [128, 4096], F32, 172 * KB)
        xr_full7 = _sb_at(nc, "xr7", [128, 4096], F32, 188 * KB)

        def ln_stripe7(st8, grp, gdg):
            st = grp * 8 + st8
            nonlocal xp_cons, xr_st
            p.w("Q", v7, xp_cons)
            p.w("Q", g7, gdg)
            dma("Q", xp_full7[:], x2pre[st * 128:(st + 1) * 128, :], q7)
            xld = q7.n
            p.w("V", q7, xld)
            n_ = p.do("V", lambda e: e.tensor_reduce(
                stt1[:, 0:1], xp_full7[:], AX.X, ALU.add), v7)
            p.w("V", v7, n_)
            n_ = p.do("V", lambda e: e.scalar_tensor_tensor(
                junkb7[:], xp_full7[:], 1.0, xp_full7[:], ALU.mult, ALU.mult,
                accum_out=stt1[:, 1:2]), v7)
            p.w("V", v7, n_)
            n_ = p.do("V", lambda e: e.tensor_scalar_mul(
                stt1[:, 2:3], stt1[:, 0:1], 1.0 / 4096.0), v7)
            p.w("V", v7, n_)
            n_ = p.do("V", lambda e: e.tensor_scalar_mul(
                stt1[:, 3:4], stt1[:, 1:2], 1.0 / 4096.0), v7)
            p.w("V", v7, n_)
            n_ = p.do("V", lambda e: e.tensor_mul(
                stt2[:, 0:1], stt1[:, 2:3], stt1[:, 2:3]), v7)
            p.w("V", v7, n_)
            n_ = p.do("V", lambda e: e.tensor_sub(
                stt2[:, 1:2], stt1[:, 3:4], stt2[:, 0:1]), v7)
            p.w("V", v7, n_)
            n_ = p.do("V", lambda e: e.tensor_scalar_add(
                stt2[:, 1:2], stt2[:, 1:2], EPS), v7)
            p.w("V", v7, n_)
            n_ = p.do("V", lambda e: e.reciprocal(stt2[:, 2:3], stt2[:, 1:2]), v7)
            p.w("V", v7, n_)
            n_ = p.do("V", lambda e: e.tensor_scalar_min(
                stt2[:, 3:4], stt2[:, 2:3], 1.0), v7)
            p.w("V", v7, n_)
            for _it in range(18):
                n_ = p.do("V", lambda e: e.tensor_mul(
                    stt3[:, 0:1], stt2[:, 3:4], stt2[:, 3:4]), v7)
                p.w("V", v7, n_)
                n_ = p.do("V", lambda e: e.tensor_mul(
                    stt3[:, 0:1], stt3[:, 0:1], stt2[:, 1:2]), v7)
                p.w("V", v7, n_)
                n_ = p.do("V", lambda e: e.tensor_scalar(
                    stt3[:, 0:1], stt3[:, 0:1], -0.5, 1.5, ALU.mult, ALU.add), v7)
                p.w("V", v7, n_)
                n_ = p.do("V", lambda e: e.tensor_mul(
                    stt2[:, 3:4], stt2[:, 3:4], stt3[:, 0:1]), v7)
                p.w("V", v7, n_)
            p.w("V", q8, xr_st)
            xp_cons = p.do("V", lambda e: e.tensor_scalar(
                xr_full7[:], xp_full7[:], stt1[:, 2:3], stt2[:, 3:4],
                ALU.subtract, ALU.mult), v7)
            p.w("Q", v7, xp_cons)
            xr_st = 16 + q8.n
            dma("Q", out_c[st * 128:(st + 1) * 128, :], xr_full7[:], q8)
        pending_ln7 = []
        x1t_cons = 0
        qw_cons = [0, 0]
        bank_ev7 = [0, 0, 0, 0]
        fb_cons = [0, 0]
        x1s_cons = [0, 0]
        x2c_st = [0, 0]
        xp_cons = 0
        xr_st = 0
        for grp in range(2):
            for kt in range(NKT):
                eng = "Q"
                p.w(eng, t5, t5_final)
                p.w(eng, g5, g5_x1row[grp])
                if grp == 1:
                    p.w(eng, t7, x1t_cons)
                p.do(eng, lambda e, kt=kt, grp=grp: e.dma_start_transpose(
                    x1T_sb[:, kt * 1024:(kt + 1) * 1024],
                    x1row[grp * 1024:(grp + 1) * 1024, kt * 128:(kt + 1) * 128]), q7, 16)
            x1trd = q7.n
            for och in range(8):
                wb = och % 2
                p.w("Q", s6b, p6_d_final)
                p.w("Q", t5, t5_final)
                p.w("Q", t7, qw_cons[wb])
                dma("Q", qw_sb[wb][:].rearrange("p (kt n) -> p kt n", n=512),
                    qWT.rearrange("(kt p) o -> p kt o", p=128)[:, :, och * 512:(och + 1) * 512],
                    q7)
                qwld = q7.n
                p.w("G", v7, fb_cons[wb])
                dma("G", fb_sl[wb][:], bass.AP(
                    fb_i.tensor, och * 512, [[0, 128], [1, 512]]), g7)
                fbld = g7.n
                for st8 in range(8):
                    st = grp * 8 + st8
                    i7 = (och * 8 + st8) % 4
                    ldb = (och * 8 + st8) % 2
                    p.w("Q", v7, x1s_cons[ldb])
                    dma("Q", x1s_sl[ldb][:],
                        x1row[st * 128:(st + 1) * 128, och * 512:(och + 1) * 512], q7)
                    x1sld = q7.n
                    p.w("T", q7, max(x1trd, qwld))
                    p.w("T", v7, bank_ev7[i7])
                    for kt in range(NKT):
                        n = p.do("T", lambda e, kt=kt, st8=st8, wb=wb, i7=i7: e.matmul(
                            p4rot[i7],
                            x1T_sb[:, kt * 1024 + st8 * 128:kt * 1024 + (st8 + 1) * 128],
                            qw_sb[wb][:, kt * 512:(kt + 1) * 512],
                            start=(kt == 0), stop=(kt == NKT - 1)),
                            t7 if kt == NKT - 1 else None)
                        if kt == NKT - 1:
                            mmd = n
                    x1t_cons = mmd
                    qw_cons[wb] = mmd
                    xc = (och * 8 + st8) % 2
                    p.w("V", t7, mmd)
                    p.w("V", g7, fbld)
                    p.w("V", q7, x1sld)
                    p.w("V", g7, x2c_st[xc])
                    p.do("V", lambda e, i7=i7, wb=wb: e.tensor_add(
                        tmpf7[:], p4rot[i7], fb_sl[wb][:]))
                    ev = p.do("V", lambda e, ldb=ldb, xc=xc:
                              e.scalar_tensor_tensor(
                                  x2c_sb[xc][:], tmpf7[:], 0.0, x1s_sl[ldb][:],
                                  ALU.add, ALU.add),
                              v7)
                    bank_ev7[i7] = ev
                    x1s_cons[ldb] = ev
                    fb_cons[wb] = ev
                    p.w("G", v7, ev)
                    x2c_st[xc] = 16 + g7.n
                    dma("G", x2pre[st * 128:(st + 1) * 128, och * 512:(och + 1) * 512],
                        x2c_sb[xc][:], g7)
                    if pending_ln7 and st8 == 7:
                        ln_stripe7(*pending_ln7.pop(0))
            grp_done_v = v7.n
            grp_done_s = s7.n
            grp_done_g = g7.n
            if grp == 0:
                pending_ln7 = [(st8, grp, grp_done_g) for st8 in range(8)]
            else:
                for args in pending_ln7:
                    ln_stripe7(*args)
                pending_ln7 = []
                for st8 in range(8):
                    ln_stripe7(st8, 1, grp_done_g)

    for s in p.sems:
        if s.n > 0:
            p.w("Q", s, s.n)

    with nc.Block() as block:
        @block.tensor
        def _(e):
            for f in p.st["T"]:
                f(e)

        @block.vector
        def _(e):
            for f in p.st["V"]:
                f(e)

        @block.scalar
        def _(e):
            for f in p.st["S"]:
                f(e)

        @block.sync
        def _(e):
            for f in p.st["Q"]:
                f(e)

        @block.gpsimd
        def _(e):
            for f in p.st["G"]:
                f(e)

    es.close()
    return nc


_BF = ml_dtypes.bfloat16


def _host_tables():
    inv = (THETA ** (-np.arange(64, dtype=np.float64) / 64.0))
    pos = np.arange(S, dtype=np.float64)
    ang = inv[:, None] * pos[None, :]
    cosk = np.cos(ang).astype(np.float32)
    sink = np.sin(ang).astype(np.float32)
    tabKA = np.concatenate([cosk, cosk], axis=0)
    tabKB = np.concatenate([sink, sink], axis=0)
    return tabKA, tabKB, cosk, sink


def _host_qtables(stripes, cosk, sink):
    cols = np.concatenate([np.arange(p * ST, (p + 1) * ST) for p in stripes])
    sc = np.float32(1.0 / np.sqrt(D))
    cq = cosk[:, cols] * sc
    sq = sink[:, cols] * sc
    return (np.concatenate([cq, cq], axis=0).astype(np.float32),
            np.concatenate([sq, sq], axis=0).astype(np.float32))


def _perm_cols(W, npair):
    r = W.reshape(W.shape[0], npair, 2, 2, 64)
    return np.ascontiguousarray(r.transpose(0, 1, 3, 2, 4).reshape(W.shape))


def _perm_bias(bvec, npair):
    b = np.asarray(bvec, np.float32).reshape(npair, 2, 2, 64)
    return (np.ascontiguousarray(b[:, :, 0, :].reshape(npair, 128)),
            np.ascontiguousarray(b[:, :, 1, :].reshape(npair, 128)))


def _host_masks(stripes):
    tri = np.triu(np.ones((128, 128), np.float32))
    m = np.zeros((MYST, 2, 128, 128), np.float32)
    for j, pos in enumerate(stripes):
        if pos == 2 * j + 1:
            m[j, 0] = 1.0
            m[j, 1] = tri
        else:
            m[j, 0] = tri
            m[j, 1] = 0.0
    return np.ascontiguousarray(m.transpose(2, 0, 1, 3).reshape(128, MYST * 2 * 128)
                                ).astype(_BF)


_CACHE = {}


def _get_nc():
    if "nc" not in _CACHE:
        _CACHE["nc"] = build_program()
    return _CACHE["nc"]


def make_in_maps(L0_attn_hidden_states, L0_attn_Wq, L0_attn_bq, L0_attn_Wk,
                 L0_attn_bk, L0_attn_Wv, L0_attn_bv, L0_attn_Wo, L0_attn_bo,
                 L0_ffn_weight, L0_ffn_bias):
    x = np.ascontiguousarray(np.asarray(L0_attn_hidden_states, dtype=np.float32))
    tabKA, tabKB, cosk, sink = _host_tables()
    bqA, bqB = _perm_bias(L0_attn_bq, 16)
    bkA, bkB = _perm_bias(L0_attn_bk, 4)
    shared = {
        "Wq": _perm_cols(np.asarray(L0_attn_Wq, np.float32), 16),
        "Wk": _perm_cols(np.asarray(L0_attn_Wk, np.float32), 4),
        "Wv": np.ascontiguousarray(np.asarray(L0_attn_Wv, np.float32)),
        "Wo": np.ascontiguousarray(np.asarray(L0_attn_Wo, np.float32)),
        "Wf": np.ascontiguousarray(np.asarray(L0_ffn_weight, np.float32)),
        "bqA_r": bqA, "bqB_r": bqB, "bkA_r": bkA, "bkB_r": bkB,
        "bv_i": np.asarray(L0_attn_bv, np.float32).reshape(1, -1),
        "bo_i": np.asarray(L0_attn_bo, np.float32).reshape(1, -1),
        "fb_i": np.asarray(L0_ffn_bias, np.float32).reshape(1, -1),
        "tabKA": tabKA, "tabKB": tabKB,
        "identf": np.eye(128, dtype=np.float32),
        "onescol_i": np.ones((128, 1), _BF),
        "onesrow_bf_i": np.ones((1, 128), _BF),
        "onesrow_f_i": np.ones((1, 128), np.float32),
    }
    sets = [SET_A, SET_B]
    qtabs = [_host_qtables(s, cosk, sink) for s in sets]
    masks = [_host_masks(s) for s in sets]
    in_maps = []
    for core in range(8):
        b = core // 2
        h = core % 2
        stripes = sets[h]
        xb = x[b]
        xm = np.ascontiguousarray(
            xb.reshape(NST, ST, HID)[stripes].reshape(MYST * ST, HID))
        m = dict(shared)
        m["x_full"] = xb
        m["x_mine"] = xm
        m["tabQA"], m["tabQB"] = qtabs[h]
        m["dmask_i"] = masks[h]
        in_maps.append(m)
    return in_maps


def kernel(**inputs):
    in_maps = make_in_maps(**inputs)
    nc = _get_nc()
    res = run_bass_kernel_spmd(nc, in_maps, core_ids=list(range(8)))
    out = np.empty((B, S, HID), dtype=np.float32)
    sets = [SET_A, SET_B]
    for core in range(8):
        b = core // 2
        h = core % 2
        stripes = sets[h]
        oc = np.asarray(res.results[core]["out_c"]).reshape(MYST, ST, HID)
        for j, pos in enumerate(stripes):
            out[b, pos * ST:(pos + 1) * ST, :] = oc[j]
    return out
